# version 15 (final)
# speedup vs baseline: 1.0296x; 1.0008x over previous
import math
import os
from contextlib import ExitStack

import numpy as np
import ml_dtypes

import concourse.bass as bass
import concourse.mybir as mybir
from concourse.bass_utils import run_bass_kernel_spmd

F32 = mybir.dt.float32
BF16 = mybir.dt.bfloat16
AF = mybir.ActivationFunctionType
ALU = mybir.AluOpType
AX = mybir.AxisListType

P = 128
D = 1024
DT = 8
DI = 2048
DIT = 16
DIH = 1024
DHT = 8
NS = 16
T = 4096
TO = 2048
TC = 512
NCH = 8
NOCH = 4
FF = 4096
FFT = 32
MAGIC = 12582912.0
EPS = 1e-5
GAMMA = 1.0 / 32.0

FP64_MODE = os.environ.get("BASS_SIM_FP64") == "1"
PH = int(os.environ.get("MK_PHASES", "9"))
DBG = int(os.environ.get("MK_DBG", "0"))

CW_SLAB_LO = 0
CW_SLAB_HI = 4
CW_PARTNER = 8
CW_OWN = 12
CW_MERGE_W = 14
CW_MERGE_B = 15


class Sem:
    def __init__(self, h):
        self.h = h
        self.n = 0

    def inc(self, instr, k=1):
        instr.then_inc(self.h, k)
        self.n += k
        return self.n


def _pwl_coeffs():
    xs = [(-4.0 + 0.25 * i) for i in range(33)]
    ys = [0.5 * x * (1.0 + math.erf(x / math.sqrt(2.0))) for x in xs]
    s = [(ys[i + 1] - ys[i]) * 4.0 for i in range(32)]
    a0 = ys[0] * 256.0
    knots = [(-1024.0, s[0])]
    knots += [(-1024.0 + 64.0 * j, s[j] - s[j - 1]) for j in range(1, 32)]
    knots += [(1024.0, 1.0 - s[31])]
    return a0, knots


PWL_A0, PWL_KNOTS = _pwl_coeffs()


def build_nc():
    nc = bass.Bass("TRN2", target_bir_lowering=False)

    def din(name, shape, dt=F32):
        return nc.dram_tensor(name, shape, dt, kind="ExternalInput").ap()

    x_in = din("x_in", [TO, D])
    w_in_u = din("w_in_u", [D, DI])
    w_in_res = din("w_in_res", [D, DI])
    w_delta = din("w_delta", [DI, DIH])
    b_delta_h = din("b_delta_h", [P, DHT])
    w_b = din("w_b", [DI, NS])
    w_c = din("w_c", [DI, NS])
    a_log_h = din("a_log_h", [P, DHT * NS])
    d_param = din("d_param", [P, DIT])
    w_out = din("w_out", [DI, D])
    ffn_w1 = din("ffn_w1", [D, FF])
    ffn_b1 = din("ffn_b1", [P, FFT])
    ffn_w2 = din("ffn_w2", [FF, D])
    ffn_b2 = din("ffn_b2", [P, DT])
    wqkv = din("wqkv", [3 * DT * P, P])
    rope_cos = din("rope_cos", [D // 2, T])
    rope_sin = din("rope_sin", [D // 2, T])
    rope_cos_own = din("rope_cos_own", [D // 2, TO])
    rope_sin_own = din("rope_sin_own", [D // 2, TO])
    tri_m = din("tri_m", [P, P])
    id_in = din("id_in", [P, P])
    cw_in = din("cw_in", [P, 16])
    sel16_in = din("sel16_in", [NS, NS * P])

    x_out = nc.dram_tensor("x_out", [TO, D], F32, kind="ExternalOutput").ap()

    def dram(name, shape, dt=F32):
        return nc.dram_tensor(name, shape, dt).ap()

    cc1_in = dram("cc1_in", [D, TO], BF16)
    cc1_out = dram("cc1_out", [4 * D, TO], BF16)
    cc2_in = dram("cc2_in", [DIH, TO], BF16)
    cc2_out = dram("cc2_out", [4 * DIH, TO], BF16)
    cc3_in = dram("cc3_in", [D, TO], BF16)
    cc3_out = dram("cc3_out", [4 * D, TO], BF16)

    x1sp = dram("x1sp", [D, TO])
    x2sp = dram("x2sp", [D, TO])
    x3sp = dram("x3sp", [D, TO])
    x4sp = dram("x4sp", [D, TO])
    u_all = dram("u_all", [DI, T], BF16)
    sres_d = dram("sres_d", [DI, TO], BF16)
    y_all = dram("y_all", [DIH, T], BF16)
    qown_d = dram("qown_d", [D, TO], BF16)
    qodd_d = dram("qodd_d", [D, TO], BF16)
    atri = dram("atri", [TO, D + 2])
    arect = dram("arect", [TO, D + 2])
    dsp = dram("dsp", [DIH, T], BF16)
    dusp = dram("dusp", [DIH, T], BF16)
    bc_d = dram("bc_d", [NS, 2 * T], BF16)

    dbgo = {}
    if DBG >= 1:
        for nm, shape, dt in [("dbg_xn1", [D, TO], BF16), ("dbg_x2", [D, TO], F32),
                              ("dbg_u", [DI, T], BF16), ("dbg_y", [DIH, T], F32),
                              ("dbg_x3", [D, TO], F32), ("dbg_x4", [D, TO], F32)]:
            dbgo[nm] = nc.dram_tensor(nm, shape, dt, kind="ExternalOutput").ap()

    es = ExitStack()
    with es:
        def sb(name, shape, dt=F32):
            return es.enter_context(nc.sbuf_tensor(name, shape, dt))

        def psum(name, shape, dt=F32):
            return es.enter_context(nc.psum_tensor(name, shape, dt))

        def mksem(name):
            return Sem(es.enter_context(nc.semaphore(name)))

        blk = es.enter_context(nc.Block())

        ones1f = sb("ones1f", [1, P], F32)
        onesP = sb("onesP", [P, 1], BF16)
        idf = sb("idf", [P, P], F32)
        idb = sb("idb", [P, P], BF16)
        cw = sb("cw", [P, 16])
        consts = sb("consts", [P, 24])
        trim = sb("trim", [P, P])
        scrA = sb("scrA", [P, TC])
        scrB = sb("scrB", [P, TC])
        scrC = sb("scrC", [P, TC])
        scrD = sb("scrD", [P, TC])
        colA = sb("colA", [P, 16])
        rowA = None

        _cmap = {}

        def C(val, parts=P):
            return consts[0:parts, _cmap[float(val)]:_cmap[float(val)] + 1]

        def cwc(col, parts=P):
            return cw[0:parts, col:col + 1]

        s_v = mksem("s_v")
        s_a = mksem("s_a")
        s_t = mksem("s_t")
        s_p = mksem("s_p")
        s_cc = mksem("s_cc")
        sd = [mksem(f"sd{i}") for i in range(24)]
        gd = [mksem(f"gd{i}") for i in range(16)]

        pA = psum("pA", [P, TC])
        pB = psum("pB", [P, TC])
        pC = psum("pC", [P, TC])
        pD = psum("pD", [P, TC])
        pE = psum("pE", [P, TC])
        pF = psum("pF", [P, TC])
        pR = psum("pR", [1, TC])
        pR2 = psum("pR2", [1, TC])
        BANKS = [pA, pB, pC, pD]

        CONST_VALS = [EPS, -0.5, 1.0, -1.0, 256.0, 1.0 / 256.0, -MAGIC / 256.0,
                      GAMMA, 0.5, 2.0, -2.0, 0.0, MAGIC, -MAGIC, 1.0 / D, 8192.5]

        @blk.sync
        def _(q):
            sd[0].inc(q.dma_start(out=cw[:, :], in_=cw_in), 16)
            sd[0].inc(q.dma_start(out=idf[:, :], in_=id_in), 16)
            sd[0].inc(q.dma_start(out=trim[:, :], in_=tri_m), 16)
        boot_sd0 = sd[0].n

        @blk.gpsimd
        def _(g):
            gd[0].inc(g.dma_start(out=idb[:, :], in_=id_in), 16)
        boot_gd0 = gd[0].n

        @blk.vector
        def _(v):
            s_v.inc(v.memset(ones1f[:, :], 1.0))
            s_v.inc(v.memset(onesP[:, :], 1.0))
            for i, val in enumerate(CONST_VALS):
                _cmap[float(val)] = i
                s_v.inc(v.memset(consts[:, i:i + 1], val))
        boot_v = s_v.n

        def vw(v):
            v.wait_ge(s_v.h, s_v.n)

        def aw(sc):
            sc.wait_ge(s_a.h, s_a.n)

        def barrier_marks():
            return (s_v.n, s_a.n, s_t.n, s_p.n,
                    [x.n for x in sd], [x.n for x in gd], s_cc.n)

        def wait_marks(eng, mk):
            nv, na, nt, np_, nsd, ngd, ncc = mk
            eng.wait_ge(s_v.h, nv)
            eng.wait_ge(s_a.h, na)
            eng.wait_ge(s_t.h, nt)
            eng.wait_ge(s_p.h, np_)
            for s, n in zip(sd, nsd):
                if n:
                    eng.wait_ge(s.h, n)
            for s, n in zip(gd, ngd):
                if n:
                    eng.wait_ge(s.h, n)
            eng.wait_ge(s_cc.h, ncc)

        def phase_barrier():
            mk = barrier_marks()

            @blk.sync
            def _(q):
                wait_marks(q, mk)

            @blk.gpsimd
            def _(g):
                wait_marks(g, mk)

            @blk.vector
            def _(v):
                wait_marks(v, mk)

            @blk.scalar
            def _(sc):
                wait_marks(sc, mk)

            @blk.tensor
            def _(t):
                wait_marks(t, mk)

        def proj(src_tile, w_tile, n_m, n_k, evac, t_wait=None, groups=2,
                 bank_list=None):
            bl = bank_list or BANKS
            bank_of = lambda m: bl[m % groups]
            evac_marks = {}
            for m0 in range(0, n_m, groups):
                ms = list(range(m0, min(m0 + groups, n_m)))
                t_mark0 = s_t.n

                @blk.tensor
                def _(t, ms=ms):
                    if t_wait is not None:
                        t_wait(t)
                    for m in ms:
                        if m >= groups:
                            em = evac_marks[m - groups]
                            t.wait_ge(em[0].h, em[1])
                        for k in range(n_k):
                            s_t.inc(t.matmul(bank_of(m)[:, :], w_tile(k, m),
                                             src_tile(k),
                                             start=(k == 0), stop=(k == n_k - 1)))

                @blk.scalar
                def _(sc, ms=ms, t_mark0=t_mark0):
                    for j, m in enumerate(ms):
                        sc.wait_ge(s_t.h, t_mark0 + (j + 1) * n_k)
                        aw(sc)
                        evac(sc, m, bank_of(m))
                        evac_marks[m] = (s_a, s_a.n)

        def emit_ln(xoff, xbuf, xnbuf, sqbuf, pre_wait=None):
            @blk.vector
            def _(v):
                if pre_wait is not None:
                    pre_wait(v)
                for d in range(0, DT, 2):
                    vw(v)
                    s_v.inc(v.tensor_copy(xnbuf[:, d * TC:(d + 2) * TC],
                                          xbuf[:, xoff(d):xoff(d) + 2 * TC]))
            vcast = s_v.n

            @blk.scalar
            def _(sc):
                if pre_wait is not None:
                    pre_wait(sc)
                for d in range(DT):
                    aw(sc)
                    s_a.inc(sc.activation(sqbuf[:, d * TC:(d + 1) * TC],
                                          xbuf[:, xoff(d):xoff(d) + TC], AF.Square))
            acast = s_a.n

            @blk.tensor
            def _(t):
                t.wait_ge(s_v.h, vcast)
                for d in range(DT):
                    s_t.inc(t.matmul(pR[0:1, 0:TC], onesP[:, :],
                                     xnbuf[:, d * TC:(d + 1) * TC],
                                     start=(d == 0), stop=(d == DT - 1)))
                t.wait_ge(s_a.h, acast)
                for d in range(DT):
                    s_t.inc(t.matmul(pR2[0:1, 0:TC], onesP[:, :],
                                     sqbuf[:, d * TC:(d + 1) * TC],
                                     start=(d == 0), stop=(d == DT - 1)))
            tsum = s_t.n

            @blk.vector
            def _(v):
                v.wait_ge(s_t.h, tsum - DT)
                vw(v)
                s_v.inc(v.tensor_single_scalar(rowA[0:1, 0:TC], pR[0:1, 0:TC],
                                               1.0 / D, ALU.mult))
                v.wait_ge(s_t.h, tsum)
                vw(v)
                s_v.inc(v.tensor_single_scalar(rowA[0:1, TC:2 * TC], pR2[0:1, 0:TC],
                                               1.0 / D, ALU.mult))
                vw(v)
                s_v.inc(v.tensor_tensor(rowA[0:1, 2 * TC:3 * TC], rowA[0:1, 0:TC],
                                        rowA[0:1, 0:TC], ALU.mult))
                vw(v)
                s_v.inc(v.tensor_tensor(rowA[0:1, 3 * TC:4 * TC], rowA[0:1, TC:2 * TC],
                                        rowA[0:1, 2 * TC:3 * TC], ALU.subtract))
            vrow = s_v.n

            @blk.scalar
            def _(sc):
                sc.wait_ge(s_v.h, vrow)
                aw(sc)
                s_a.inc(sc.activation(rowA[0:1, 4 * TC:5 * TC],
                                      rowA[0:1, 3 * TC:4 * TC], AF.Ln, bias=C(EPS, 1)))
                aw(sc)
                s_a.inc(sc.activation(rowA[0:1, 3 * TC:4 * TC],
                                      rowA[0:1, 4 * TC:5 * TC], AF.Exp,
                                      scale=C(-0.5, 1)))
            arow = s_a.n

            @blk.vector
            def _(v):
                v.wait_ge(s_a.h, arow)
                vw(v)
                s_v.inc(v.tensor_tensor(rowA[0:1, 5 * TC:6 * TC], rowA[0:1, 0:TC],
                                        rowA[0:1, 3 * TC:4 * TC], ALU.mult))
            vmr = s_v.n

            @blk.tensor
            def _(t):
                t.wait_ge(s_v.h, vmr)
                s_t.inc(t.matmul(pE[:, 0:TC], ones1f[0:1, :],
                                 rowA[0:1, 3 * TC:4 * TC], start=True, stop=True))
                s_t.inc(t.matmul(pF[:, 0:TC], ones1f[0:1, :],
                                 rowA[0:1, 5 * TC:6 * TC], start=True, stop=True))
            tbc = s_t.n

            @blk.vector
            def _(v):
                v.wait_ge(s_t.h, tbc)
                for d in range(DT):
                    vw(v)
                    s_v.inc(v.tensor_tensor(xnbuf[:, d * TC:(d + 1) * TC],
                                            xbuf[:, xoff(d):xoff(d) + TC],
                                            pE[:, 0:TC], ALU.mult))
                    vw(v)
                    s_v.inc(v.tensor_tensor(xnbuf[:, d * TC:(d + 1) * TC],
                                            xnbuf[:, d * TC:(d + 1) * TC],
                                            pF[:, 0:TC], ALU.subtract))
            return s_v.n

        def emit_rne_f(v, out_ap, in_ap, cols, post_mul=None, out2=None):
            if not FP64_MODE:
                if post_mul is None:
                    vw(v)
                    s_v.inc(v.tensor_scalar(out_ap, in_ap, MAGIC, MAGIC,
                                            ALU.add, ALU.subtract))
                else:
                    vw(v)
                    s_v.inc(v.tensor_single_scalar(scrC[:, 0:cols], in_ap,
                                                   MAGIC, ALU.add))
                    vw(v)
                    s_v.inc(v.tensor_scalar(out_ap, scrC[:, 0:cols], MAGIC,
                                            post_mul, ALU.subtract, ALU.mult))
            else:
                vw(v)
                s_v.inc(v.tensor_single_scalar(scrB[:, 0:cols], in_ap,
                                               8192.5, ALU.add))
                for bit in range(13, -1, -1):
                    val = float(1 << bit)
                    vw(v)
                    s_v.inc(v.tensor_scalar(scrC[:, 0:cols], scrB[:, 0:cols],
                                            val, val, ALU.is_ge, ALU.mult))
                    vw(v)
                    s_v.inc(v.tensor_tensor(scrB[:, 0:cols], scrB[:, 0:cols],
                                            scrC[:, 0:cols], ALU.subtract))
                vw(v)
                s_v.inc(v.tensor_single_scalar(scrC[:, 0:cols], in_ap,
                                               8192.5, ALU.add))
                vw(v)
                s_v.inc(v.tensor_tensor(scrC[:, 0:cols], scrC[:, 0:cols],
                                        scrB[:, 0:cols], ALU.subtract))
                if post_mul is None:
                    vw(v)
                    s_v.inc(v.tensor_single_scalar(out_ap, scrC[:, 0:cols],
                                                   -8192.0, ALU.add))
                else:
                    vw(v)
                    s_v.inc(v.tensor_scalar(out_ap, scrC[:, 0:cols], -8192.0,
                                            post_mul, ALU.add, ALU.mult))

        xnb_es = ExitStack()
        xnb = xnb_es.enter_context(nc.sbuf_tensor("xnb", [P, DT * TO], BF16))
        p01es = ExitStack()
        sq16 = p01es.enter_context(nc.sbuf_tensor("sq16", [P, DT * TC], BF16))
        xn16 = p01es.enter_context(nc.sbuf_tensor("xn16", [P, DT * TC], BF16))
        rowA = p01es.enter_context(nc.sbuf_tensor("rowA", [1, 8 * TC], F32))

        with nc.sbuf_tensor("xbig", [P, 2 * D], F32) as xbig, \
             nc.sbuf_tensor("xev", [P, DT * P], F32) as xev, \
             nc.sbuf_tensor("xchunk", [P, DT * TC], F32) as xchunk:

            NXC = TO // P
            ev_done = {}
            sp_done = {}
            for i in range(NXC):
                par = i % 2
                ld0 = sd[1].n

                @blk.sync
                def _(q, i=i, par=par):
                    if i >= 2:
                        q.wait_ge(s_t.h, tr_done[i - 2])
                    sd[1].inc(q.dma_start(
                        out=xbig[:, par * D:(par + 1) * D],
                        in_=x_in[i * P:(i + 1) * P, :]), 16)
                ld1 = sd[1].n

                if i == 0:
                    tr_done = {}
                    ev_k = {}
                for d in range(DT):
                    k = i * DT + d

                    @blk.tensor
                    def _(t, i=i, d=d, k=k, par=par, ld1=ld1):
                        if d == 0:
                            t.wait_ge(sd[1].h, ld1)
                            if i == 0:
                                t.wait_ge(sd[0].h, boot_sd0)
                        if k >= 2:
                            t.wait_ge(s_v.h, ev_k[k - 2])
                        s_t.inc(t.transpose(
                            (pA if k % 2 == 0 else pB)[:, 0:P],
                            xbig[:, par * D + d * P:par * D + (d + 1) * P],
                            idf[:, :]))
                    tmk = s_t.n
                    tr_done[i] = s_t.n

                    @blk.vector
                    def _(v, d=d, k=k, tmk=tmk, i=i):
                        v.wait_ge(s_t.h, tmk)
                        if i >= 1:
                            v.wait_ge(sd[2].h, sp_done[i - 1])
                        vw(v)
                        s_v.inc(v.tensor_copy(
                            xev[:, d * P:(d + 1) * P],
                            (pA if k % 2 == 0 else pB)[:, 0:P]))
                    ev_k[k] = s_v.n
                ev_done[i] = s_v.n

                @blk.sync
                def _(q, i=i):
                    q.wait_ge(s_v.h, ev_done[i])
                    sd[2].inc(q.dma_start(
                        out=x1sp.rearrange("(d p) t -> p d t", p=P)[
                            :, :, i * P:(i + 1) * P],
                        in_=xev[:, :].rearrange("p (d t) -> p d t", t=P)), 16)
                sp_done[i] = sd[2].n
            sp_x1 = sd[2].n

            for c in range(NOCH):
                cs = c * TC
                ldc0 = sd[1].n

                @blk.sync
                def _(q, cs=cs, c=c):
                    q.wait_ge(sd[2].h, sp_x1)
                    if c >= 1:
                        q.wait_ge(s_v.h, xch_used[0])
                    sd[1].inc(q.dma_start(
                        out=xchunk[:, :].rearrange("p (d t) -> p d t", t=TC),
                        in_=x1sp.rearrange("(d p) t -> p d t", p=P)[
                            :, :, cs:cs + TC]), 16)
                ldc1 = sd[1].n

                prev_sd3 = sd[3].n
                vxn = emit_ln(lambda d: d * TC, xchunk, xn16, sq16,
                              pre_wait=(lambda e, ldc1=ldc1, prev_sd3=prev_sd3:
                                        (e.wait_ge(sd[1].h, ldc1),
                                         e.wait_ge(sd[3].h, prev_sd3))))
                if c == 0:
                    xch_used = [0]
                xch_used[0] = vxn

                @blk.vector
                def _(v, cs=cs, vxn=vxn):
                    v.wait_ge(s_v.h, vxn)
                    for d in range(DT):
                        vw(v)
                        s_v.inc(v.tensor_copy(xnb[:, d * TO + cs:d * TO + cs + TC],
                                              xn16[:, d * TC:(d + 1) * TC]))
                vkeep = s_v.n

                @blk.sync
                def _(q, cs=cs, vkeep=vkeep):
                    q.wait_ge(s_v.h, vkeep)
                    sd[3].inc(q.dma_start(
                        out=cc1_in[:, cs:cs + TC].rearrange("(d p) t -> p d t", p=P),
                        in_=xn16[:, :].rearrange("p (d t) -> p d t", t=TC)), 16)
            ln1_sp = sd[3].n

            if DBG >= 1:
                @blk.gpsimd
                def _(g):
                    g.wait_ge(sd[3].h, ln1_sp)
                    gd[1].inc(g.dma_start(out=dbgo["dbg_xn1"], in_=cc1_in), 16)

            @blk.gpsimd
            def _(g):
                g.wait_ge(sd[3].h, ln1_sp)
                if PH >= 2:
                    s_cc.inc(g.collective_compute(
                        "AllGather", ALU.bypass,
                        replica_groups=[[0, 1, 2, 3], [4, 5, 6, 7]],
                        ins=[cc1_in.tensor.ap().opt()],
                        outs=[cc1_out.tensor.ap().opt()]))
            cc1_done = s_cc.n
        p01es.close()

        def _phase2a():
            les = ExitStack()
            wres = les.enter_context(nc.sbuf_tensor("wres", [P, DT * DIT * P], BF16))
            sres16 = les.enter_context(nc.sbuf_tensor("sres16", [P, DIT * TC], BF16))
            if True:

                @blk.gpsimd
                def _(g):
                    g.wait_ge(sd[3].h, ln1_sp)
                    g.wait_ge(s_v.h, s_v.n)
                    g.wait_ge(s_t.h, s_t.n)
                    gd[2].inc(g.dma_start(
                        out=wres[:, :].rearrange("p (k m c) -> p k m c", m=DIT, c=P),
                        in_=w_in_res.rearrange("(k p) (m c) -> p k m c", p=P, c=P)),
                        16)
                ld_wres = gd[2].n

                for c in range(NOCH):
                    cs = c * TC
                    sp_mark = sd[4].n

                    def evac_sres(sc, m, bank):
                        s_a.inc(sc.activation(sres16[:, m * TC:(m + 1) * TC],
                                              bank[:, :], AF.Silu))

                    proj(src_tile=lambda k, cs=cs: xnb[:, k * TO + cs:k * TO + cs + TC],
                         w_tile=lambda k, m: wres[:, (k * DIT + m) * P:
                                                  (k * DIT + m + 1) * P],
                         n_m=DIT, n_k=DT, evac=evac_sres,
                         t_wait=(lambda t, sp_mark=sp_mark:
                                 (t.wait_ge(gd[2].h, ld_wres),
                                  t.wait_ge(sd[4].h, sp_mark))))
                    a1 = s_a.n

                    @blk.sync
                    def _(q, cs=cs, a1=a1):
                        q.wait_ge(s_a.h, a1)
                        sd[4].inc(q.dma_start(
                            out=sres_d[:, cs:cs + TC].rearrange(
                                "(m p) t -> p m t", p=P),
                            in_=sres16[:, :].rearrange("p (m t) -> p m t", t=TC)), 16)
                phase_barrier()
            les.close()

        if PH >= 2:
            _phase2a()
        xnb_es.close()

        p2b_les = ExitStack()
        def _phase2b_setup():
            les = p2b_les
            def lsb(nm, shape, dt=F32):
                return les.enter_context(nc.sbuf_tensor(nm, shape, dt))
            wu = lsb("wu", [P, DT * DIT * P], BF16)
            wd = lsb("wd", [P, DIT * DHT * P], BF16)
            wbc = lsb("wbc", [P, DIT * 2 * NS], BF16)
            bdel = lsb("bdel", [P, DHT], F32)
            slabst = lsb("slabst", [P, DT * TC], BF16)
            xnc = lsb("xnc", [P, DT * TC], BF16)
            uc = lsb("uc", [P, DIT * TC], BF16)
            delc = lsb("delc", [P, DHT * TC], BF16)
            duc = lsb("duc", [P, DHT * TC], BF16)
            bcrow = lsb("bcrow", [NS, 2 * TC], BF16)
            if True:

                @blk.gpsimd
                def _(g):
                    gd[3].inc(g.dma_start(
                        out=wu[:, :].rearrange("p (k m c) -> p k m c", m=DIT, c=P),
                        in_=w_in_u.rearrange("(k p) (m c) -> p k m c", p=P, c=P)), 16)
                    gd[3].inc(g.dma_start(
                        out=wd[:, :].rearrange("p (k m c) -> p k m c", m=DHT, c=P),
                        in_=w_delta.rearrange("(k p) (m c) -> p k m c", p=P, c=P)), 16)
                    gd[3].inc(g.dma_start(
                        out=wbc[:, 0:DIT * NS].rearrange("p (k n) -> p k n", n=NS),
                        in_=w_b.rearrange("(k p) n -> p k n", p=P)), 16)
                    gd[3].inc(g.dma_start(
                        out=wbc[:, DIT * NS:].rearrange("p (k n) -> p k n", n=NS),
                        in_=w_c.rearrange("(k p) n -> p k n", p=P)), 16)
                    gd[3].inc(g.dma_start(out=bdel[:, :], in_=b_delta_h), 16)
                ld_wscan = gd[3].n

                xnc_consumed = {}
                del_tile_done = {}
                bc_ev_mark = {}
                dsp_st = [0]
                dusp_st = [0]
                bc_st = [0]
                dsp_st_c = {}
                dusp_bc_c = {}
                slab_used_v = [0]

                def emit_proj_chunk(c):
                    cs = c * TC
                    hsel = CW_SLAB_LO if c < NCH // 2 else CW_SLAB_HI
                    ccs = cs if c < NCH // 2 else cs - TO
                    for s in range(4):
                        @blk.sync
                        def _(q, s=s, c=c, ccs=ccs):
                            if s == 0:
                                q.wait_ge(s_cc.h, cc1_done)
                            if c + s > 0:
                                q.wait_ge(s_v.h, slab_used_v[0])
                            sd[6].inc(q.dma_start(
                                out=slabst[:, :].rearrange("p (d t) -> p d t", t=TC),
                                in_=cc1_out.rearrange("(s d p) t -> p s d t",
                                                      p=P, d=DT)[
                                    :, s, :, ccs:ccs + TC]), 16)
                        ld1 = sd[6].n

                        @blk.vector
                        def _(v, s=s, ld1=ld1, hsel=hsel):
                            v.wait_ge(sd[6].h, ld1)
                            for d in range(0, DT, 2):
                                vw(v)
                                if s == 0:
                                    s_v.inc(v.tensor_scalar(
                                        xnc[:, d * TC:(d + 2) * TC],
                                        slabst[:, d * TC:(d + 2) * TC],
                                        cwc(hsel + s), None, ALU.mult))
                                else:
                                    s_v.inc(v.scalar_tensor_tensor(
                                        xnc[:, d * TC:(d + 2) * TC],
                                        slabst[:, d * TC:(d + 2) * TC],
                                        cwc(hsel + s), xnc[:, d * TC:(d + 2) * TC],
                                        ALU.mult, ALU.add))
                        slab_used_v[0] = s_v.n
                    v_xnc = s_v.n

                    uall_mark = sd[7].n

                    def evac_u(sc, m, bank):
                        s_a.inc(sc.activation(uc[:, m * TC:(m + 1) * TC],
                                              bank[:, :], AF.Silu))

                    proj(src_tile=lambda k: xnc[:, k * TC:(k + 1) * TC],
                         w_tile=lambda k, m: wu[:, (k * DIT + m) * P:
                                                (k * DIT + m + 1) * P],
                         n_m=DIT, n_k=DT, evac=evac_u,
                         t_wait=(lambda t, v_xnc=v_xnc, uall_mark=uall_mark, c=c:
                                 (t.wait_ge(gd[3].h, ld_wscan),
                                  t.wait_ge(s_v.h, v_xnc),
                                  t.wait_ge(sd[7].h, uall_mark),
                                  (t.wait_ge(s_v.h, del_tile_done[DHT - 1])
                                   if c >= 1 else None),
                                  (t.wait_ge(s_v.h, bc_ev_mark[c - 1])
                                   if c >= 1 else None))))
                    xnc_consumed[c] = s_t.n
                    u_ev = s_a.n

                    @blk.sync
                    def _(q, cs=cs, u_ev=u_ev):
                        q.wait_ge(s_a.h, u_ev)
                        sd[7].inc(q.dma_start(
                            out=u_all[:, cs:cs + TC].rearrange("(m p) t -> p m t",
                                                               p=P),
                            in_=uc[:, :].rearrange("p (m t) -> p m t", t=TC)), 16)

                    for m in range(DHT):
                        bank = pA if m % 2 == 0 else pB
                        @blk.tensor
                        def _(t, m=m, bank=bank, u_ev=u_ev, c=c):
                            t.wait_ge(s_a.h, u_ev)
                            if m >= 2:
                                t.wait_ge(s_v.h, del_tile_done[m - 2])
                            elif c >= 1 and m == 0:
                                t.wait_ge(s_v.h, del_tile_done[DHT - 2])
                            elif c >= 1 and m == 1:
                                t.wait_ge(s_v.h, del_tile_done[DHT - 1])
                            for k in range(DIT):
                                s_t.inc(t.matmul(
                                    bank[:, :],
                                    wd[:, (k * DHT + m) * P:(k * DHT + m + 1) * P],
                                    uc[:, k * TC:(k + 1) * TC],
                                    start=(k == 0), stop=(k == DIT - 1)))
                        t_d1 = s_t.n

                        @blk.vector
                        def _(v, m=m, bank=bank, t_d1=t_d1, c=c):
                            v.wait_ge(s_t.h, t_d1)
                            vw(v)
                            s_v.inc(v.tensor_scalar(scrB[:, 0:TC], bank[:, :],
                                                    bdel[:, m:m + 1], 0.0,
                                                    ALU.add, ALU.max))
                            vw(v)
                            s_v.inc(v.scalar_tensor_tensor(scrA[:, 0:TC],
                                                           scrB[:, 0:TC], -2.0,
                                                           bank[:, :],
                                                           ALU.mult, ALU.add))
                        v_d1 = s_v.n

                        @blk.scalar
                        def _(sc, m=m, v_d1=v_d1):
                            sc.wait_ge(s_v.h, v_d1)
                            aw(sc)
                            s_a.inc(sc.activation(scrC[:, 0:TC], scrA[:, 0:TC],
                                                  AF.Exp, bias=bdel[:, m:m + 1]))
                            aw(sc)
                            s_a.inc(sc.activation(scrC[:, 0:TC], scrC[:, 0:TC],
                                                  AF.Ln, bias=C(1.0)))
                        a_d1 = s_a.n

                        @blk.vector
                        def _(v, m=m, a_d1=a_d1, c=c):
                            v.wait_ge(s_a.h, a_d1)
                            if m == 0 and c >= 1:
                                v.wait_ge(gd[4].h, dsp_st[0])
                                v.wait_ge(sd[8].h, bc_st[0])
                            vw(v)
                            s_v.inc(v.tensor_tensor(delc[:, m * TC:(m + 1) * TC],
                                                    scrB[:, 0:TC], scrC[:, 0:TC],
                                                    ALU.add))
                            vw(v)
                            s_v.inc(v.tensor_tensor(duc[:, m * TC:(m + 1) * TC],
                                                    delc[:, m * TC:(m + 1) * TC],
                                                    uc[:, m * TC:(m + 1) * TC],
                                                    ALU.mult))
                        del_tile_done[m] = s_v.n
                    del_all = s_v.n

                    @blk.tensor
                    def _(t, u_ev=u_ev, c=c, del_all=del_all):
                        t.wait_ge(s_a.h, u_ev)
                        t.wait_ge(s_v.h, del_all)
                        if c >= 1:
                            t.wait_ge(s_v.h, bc_ev_mark[c - 1])
                        for which in range(2):
                            dstp = (pA if which == 0 else pB)[0:NS, 0:TC]
                            for k in range(DIT):
                                s_t.inc(t.matmul(
                                    dstp,
                                    wbc[:, (which * DIT + k) * NS:
                                        (which * DIT + k + 1) * NS],
                                    uc[:, k * TC:(k + 1) * TC],
                                    start=(k == 0), stop=(k == DIT - 1)))
                    t_bc1 = s_t.n

                    @blk.vector
                    def _(v, t_bc1=t_bc1, c=c):
                        v.wait_ge(s_t.h, t_bc1 - DIT)
                        if c >= 1:
                            v.wait_ge(sd[8].h, bc_st[0])
                        vw(v)
                        s_v.inc(v.tensor_copy(bcrow[0:NS, 0:TC], pA[0:NS, 0:TC]))
                        v.wait_ge(s_t.h, t_bc1)
                        vw(v)
                        s_v.inc(v.tensor_copy(bcrow[0:NS, TC:2 * TC], pB[0:NS, 0:TC]))
                    bc_ev_mark[c] = s_v.n
                    bc_ev = s_v.n

                    @blk.gpsimd
                    def _(g, cs=cs, del_all=del_all):
                        g.wait_ge(s_v.h, del_all)
                        gd[4].inc(g.dma_start(
                            out=dsp[:, cs:cs + TC].rearrange("(m p) t -> p m t",
                                                             p=P),
                            in_=delc[:, :].rearrange("p (m t) -> p m t", t=TC)),
                            16)
                    dsp_st[0] = gd[4].n

                    @blk.sync
                    def _(q, cs=cs, del_all=del_all, bc_ev=bc_ev):
                        q.wait_ge(s_v.h, del_all)
                        sd[8].inc(q.dma_start(
                            out=dusp[:, cs:cs + TC].rearrange("(m p) t -> p m t",
                                                              p=P),
                            in_=duc[:, :].rearrange("p (m t) -> p m t", t=TC)), 16)
                    dusp_st[0] = sd[8].n

                    @blk.sync
                    def _(q, cs=cs, bc_ev=bc_ev):
                        q.wait_ge(s_v.h, bc_ev)
                        sd[8].inc(q.dma_start(out=bc_d[0:NS, cs:cs + TC],
                                              in_=bcrow[0:NS, 0:TC]), 16)
                        sd[8].inc(q.dma_start(out=bc_d[0:NS, T + cs:T + cs + TC],
                                              in_=bcrow[0:NS, TC:2 * TC]), 16)
                    bc_st[0] = sd[8].n
                    dsp_st_c[c] = gd[4].n
                    dusp_bc_c[c] = sd[8].n

                return emit_proj_chunk, dsp_st_c, dusp_bc_c

        TCB = 1024
        NSC = T // TCB

        p2s_les = ExitStack()
        def _phase2bB_setup(dsp_st_c, dusp_bc_c):
            les = p2s_les
            def lsb(nm, shape, dt=F32):
                return les.enter_context(nc.sbuf_tensor(nm, shape, dt))
            acol = lsb("acol", [P, DHT * NS], F32)
            delc16 = lsb("delc16", [P, DHT * TCB], BF16)
            duc16 = lsb("duc16", [P, DHT * TCB], BF16)
            abuf = lsb("abuf", [P, 4 * TCB], BF16)
            bbuf = lsb("bbuf", [P, TCB], BF16)
            hbuf = lsb("hbuf", [P, 4 * TCB], BF16)
            tb2 = lsb("tb2", [P, 4 * TCB], BF16)
            btl = lsb("btl", [P, 4 * TCB], BF16)
            ctl = lsb("ctl", [P, 4 * TCB], BF16)
            ybuf = lsb("ybuf", [P, 2 * 2 * TCB], BF16)
            hstate = lsb("hstate", [P, NS * DHT], BF16)
            if True:

                @blk.sync
                def _(q):
                    sd[5].inc(q.dma_start(out=acol[:, :], in_=a_log_h), 16)
                ld_alog = sd[5].n

                @blk.scalar
                def _(sc):
                    sc.wait_ge(sd[5].h, ld_alog)
                    aw(sc)
                    s_a.inc(sc.activation(acol[:, :], acol[:, :], AF.Exp))
                a_exp = s_a.n

                @blk.vector
                def _(v):
                    v.wait_ge(s_a.h, a_exp)
                    vw(v)
                    s_v.inc(v.tensor_single_scalar(acol[:, :], acol[:, :],
                                                   -1.0, ALU.mult))
                a_neg = s_v.n

                exp_mark = {}
                scan_mark = {}
                twij = {}
                tb_rd = {}
                hstm = {}
                evacm = {}
                bm = {}
                yspg = {}
                prev_v = [0]
                prev_a_exp = [0]
                NG = DHT // 2

                def emit_bcast(sc_i, g, n, ld1):
                    sl = n % 4

                    @blk.gpsimd
                    def _(q, sc_i=sc_i, g=g, n=n, sl=sl, ld1=ld1):
                        if n >= 4:
                            q.wait_ge(s_v.h,
                                      twij[((sc_i * NG + g) * NS + n - 4)
                                           * 2 + 1])
                        elif g >= 1:
                            q.wait_ge(s_v.h,
                                      twij[((sc_i * NG + g - 1) * NS + 12 + n)
                                           * 2 + 1])
                        elif sc_i >= 1:
                            q.wait_ge(s_v.h,
                                      twij[(((sc_i - 1) * NG + NG - 1) * NS
                                            + 12 + n) * 2 + 1])
                        if g == 0 and n < 4:
                            q.wait_ge(sd[8].h, dusp_bc_c[2 * sc_i + 1])
                        cs = sc_i * TCB
                        gd[15].inc(q.dma_start(
                            out=btl[:, sl * TCB:(sl + 1) * TCB],
                            in_=bc_d[n:n + 1, cs:cs + TCB].to_broadcast(
                                (P, TCB))), 16)
                        gd[15].inc(q.dma_start(
                            out=ctl[:, sl * TCB:(sl + 1) * TCB],
                            in_=bc_d[n:n + 1, T + cs:T + cs + TCB].to_broadcast(
                                (P, TCB))), 16)
                    bm[(sc_i, g, n)] = gd[15].n

                def emit_scan_chunk(sc_i, hook):
                    cs = sc_i * TCB

                    @blk.sync
                    def _(q, sc_i=sc_i, cs=cs):
                        if sc_i >= 1:
                            q.wait_ge(s_a.h, prev_a_exp[0])
                            q.wait_ge(s_v.h, prev_v[0])
                        else:
                            q.wait_ge(s_a.h, a_exp)
                        q.wait_ge(gd[4].h, dsp_st_c[2 * sc_i + 1])
                        q.wait_ge(sd[8].h, dusp_bc_c[2 * sc_i + 1])
                        sd[5].inc(q.dma_start(
                            out=delc16[:, :].rearrange("p (m t) -> p m t", t=TCB),
                            in_=dsp.rearrange("(m p) t -> p m t", p=P)[
                                :, :, cs:cs + TCB]), 16)
                        sd[5].inc(q.dma_start(
                            out=duc16[:, :].rearrange("p (m t) -> p m t", t=TCB),
                            in_=dusp.rearrange("(m p) t -> p m t", p=P)[
                                :, :, cs:cs + TCB]), 16)
                    ld1 = sd[5].n

                    pend_spill = [None]
                    for g in range(NG):
                        gg = sc_i * NG + g
                        for n in range(4):
                            emit_bcast(sc_i, g, n, ld1)
                        if pend_spill[0] is not None:
                            pend_spill[0]()
                            pend_spill[0] = None
                        for n in range(NS):
                            sl_h = n % 2
                            sl_b = n % 4
                            for j in range(2):
                                dt_ = 2 * g + j
                                idx = (gg * NS + n) * 2 + j
                                sl_a = idx % 4

                                @blk.scalar
                                def _(sc, n=n, dt_=dt_, idx=idx, sl_a=sl_a,
                                      ld1=ld1, sc_i=sc_i, g=g, j=j):
                                    if g == 0 and n == 0 and j == 0:
                                        sc.wait_ge(sd[5].h, ld1)
                                        if sc_i == 0:
                                            sc.wait_ge(s_v.h, a_neg)
                                    if idx >= 4:
                                        sc.wait_ge(s_v.h, scan_mark[idx - 4])
                                    aw(sc)
                                    s_a.inc(sc.activation(
                                        abuf[:, sl_a * TCB:(sl_a + 1) * TCB],
                                        delc16[:, dt_ * TCB:(dt_ + 1) * TCB],
                                        AF.Exp,
                                        scale=acol[:, dt_ * NS + n:
                                                   dt_ * NS + n + 1]))
                                exp_mark[idx] = s_a.n

                            for j in range(2):
                                dt_ = 2 * g + j
                                lane = n * DHT + dt_
                                idx = (gg * NS + n) * 2 + j
                                sl_a = idx % 4
                                par = idx % 4

                                @blk.vector
                                def _(v, n=n, dt_=dt_, g=g, j=j, idx=idx,
                                      sl_a=sl_a, sl_h=sl_h, sl_b=sl_b, par=par,
                                      lane=lane, sc_i=sc_i, ld1=ld1):
                                    if g == 0 and n == 0 and j == 0:
                                        v.wait_ge(sd[5].h, ld1)
                                    v.wait_ge(gd[15].h, bm[(sc_i, g, n)])
                                    v.wait_ge(s_a.h, exp_mark[idx])
                                    if n >= 2:
                                        v.wait_ge(s_a.h,
                                                  hstm[(sc_i, n - 2, g)])
                                    elif g >= 1:
                                        v.wait_ge(s_a.h,
                                                  hstm[(sc_i, NS - 2 + n,
                                                        g - 1)])
                                    elif sc_i >= 1:
                                        v.wait_ge(s_a.h,
                                                  hstm[(sc_i - 1, NS - 2 + n,
                                                        NG - 1)])
                                    if sc_i >= 1:
                                        v.wait_ge(s_a.h,
                                                  hstm[(sc_i - 1, n, g)])
                                    if idx >= 4:
                                        v.wait_ge(s_t.h, tb_rd[idx - 4])
                                    vw(v)
                                    s_v.inc(v.tensor_tensor(
                                        bbuf[:, 0:TCB],
                                        duc16[:, dt_ * TCB:(dt_ + 1) * TCB],
                                        btl[:, sl_b * TCB:(sl_b + 1) * TCB],
                                        ALU.mult))
                                    init = (0.0 if sc_i == 0
                                            else hstate[:, lane:lane + 1])
                                    hoff = (sl_h * 2 + j) * TCB
                                    vw(v)
                                    s_v.inc(v.tensor_tensor_scan(
                                        hbuf[:, hoff:hoff + TCB],
                                        abuf[:, sl_a * TCB:(sl_a + 1) * TCB],
                                        bbuf[:, 0:TCB],
                                        init, ALU.mult, ALU.add))
                                    scan_mark[idx] = s_v.n
                                    vw(v)
                                    s_v.inc(v.tensor_tensor(
                                        tb2[:, par * TCB:(par + 1) * TCB],
                                        hbuf[:, hoff:hoff + TCB],
                                        ctl[:, sl_b * TCB:(sl_b + 1) * TCB],
                                        ALU.mult))
                                twij[idx] = s_v.n

                                bj = (pC, pD) if j == 0 else (pE, pF)

                                @blk.tensor
                                def _(t, idx=idx, gg=gg, n=n, j=j, par=par,
                                      bj=bj):
                                    t.wait_ge(s_v.h, twij[idx])
                                    if n == 0 and j == 0 and gg >= 1:
                                        t.wait_ge(s_a.h, evacm[gg - 1])
                                    for half in range(2):
                                        s_t.inc(t.matmul(
                                            bj[half][:, 0:TC], idb[:, :],
                                            tb2[:, par * TCB + half * TC:
                                                par * TCB + (half + 1) * TC],
                                            start=(n == 0), stop=(n == NS - 1)))
                                tb_rd[idx] = s_t.n

                            if n + 4 < NS:
                                emit_bcast(sc_i, g, n + 4, ld1)

                            @blk.scalar
                            def _(sc, n=n, g=g, sl_h=sl_h, sc_i=sc_i):
                                sc.wait_ge(s_v.h,
                                           scan_mark[((sc_i * NG + g) * NS + n)
                                                     * 2 + 1])
                                aw(sc)
                                s_a.inc(sc.activation(
                                    hstate[:, n * DHT + 2 * g:
                                           n * DHT + 2 * g + 2],
                                    hbuf[:, sl_h * 2 * TCB:
                                         (sl_h * 2 + 2) * TCB].rearrange(
                                        "p (j t) -> p j t", t=TCB)[
                                        :, :, TCB - 1:TCB],
                                    AF.Identity))
                            hstm[(sc_i, n, g)] = s_a.n
                            if n == 3:
                                hook()

                        last_idx = (gg * NS + NS - 1) * 2 + 1

                        @blk.scalar
                        def _(sc, g=g, gg=gg, sc_i=sc_i, last_idx=last_idx):
                            sc.wait_ge(s_t.h, tb_rd[last_idx])
                            if gg >= 2:
                                sc.wait_ge(gd[15].h, yspg[gg - 2])
                            slot = gg % 2
                            for j in range(2):
                                bj = (pC, pD) if j == 0 else (pE, pF)
                                for half in range(2):
                                    aw(sc)
                                    s_a.inc(sc.activation(
                                        ybuf[:, (slot * 2 + j) * TCB
                                             + half * TC:
                                             (slot * 2 + j) * TCB
                                             + (half + 1) * TC],
                                        bj[half][:, 0:TC], AF.Identity))
                        evacm[gg] = s_a.n

                        def _spill(g=g, gg=gg, cs=cs):
                            @blk.gpsimd
                            def _(g_, g=g, gg=gg, cs=cs):
                                g_.wait_ge(s_a.h, evacm[gg])
                                slot = gg % 2
                                gd[15].inc(g_.dma_start(
                                    out=y_all.rearrange(
                                        "(m p) t -> p m t", p=P)[
                                        :, 2 * g:2 * g + 2, cs:cs + TCB],
                                    in_=ybuf[:, slot * 2 * TCB:
                                             (slot + 1) * 2 * TCB].rearrange(
                                        "p (m t) -> p m t", t=TCB)), 16)
                            yspg[gg] = gd[15].n
                        pend_spill[0] = _spill

                    if pend_spill[0] is not None:
                        pend_spill[0]()
                        pend_spill[0] = None
                    prev_v[0] = s_v.n
                    prev_a_exp[0] = exp_mark[((sc_i * NG + NG - 1) * NS
                                              + NS - 1) * 2 + 1]

                return emit_scan_chunk

        if PH >= 2:
            _epc, _dspc, _dbcc = _phase2b_setup()
            _esc = _phase2bB_setup(_dspc, _dbcc)
            _epc(0)
            _epc(1)
            _nxt = [2]

            def _hook():
                if _nxt[0] < NCH:
                    _epc(_nxt[0])
                    _nxt[0] += 1
            for _sc in range(NSC):
                emit = _esc
                emit(_sc, _hook)
            phase_barrier()
            p2s_les.close()
            p2b_les.close()

        def _cc2():
            les = ExitStack()
            yhalf = les.enter_context(nc.sbuf_tensor("yhalf", [P, DHT * TC], BF16))
            ycmb = les.enter_context(nc.sbuf_tensor("ycmb", [P, DHT * TC], BF16))
            if True:
                for c in range(NOCH):
                    cs = c * TC
                    cc2_sp_mark = sd[9].n
                    for h in range(2):
                        ld0 = gd[4].n

                        @blk.gpsimd
                        def _(g, cs=cs, h=h, c=c):
                            if c + h > 0:
                                g.wait_ge(s_v.h, yh_used[0])
                            gd[4].inc(g.dma_start(
                                out=yhalf[:, :].rearrange("p (m t) -> p m t", t=TC),
                                in_=y_all.rearrange("(m p) t -> p m t", p=P)[
                                    :, :, h * TO + cs:h * TO + cs + TC]), 16)
                        ld1 = gd[4].n

                        @blk.vector
                        def _(v, h=h, ld1=ld1, cc2_sp_mark=cc2_sp_mark):
                            v.wait_ge(gd[4].h, ld1)
                            if h == 0:
                                v.wait_ge(sd[9].h, cc2_sp_mark)
                            wcol = CW_OWN + (1 - h)
                            for m in range(DHT):
                                vw(v)
                                if h == 0:
                                    s_v.inc(v.tensor_scalar(
                                        ycmb[:, m * TC:(m + 1) * TC],
                                        yhalf[:, m * TC:(m + 1) * TC],
                                        cwc(wcol), None, ALU.mult))
                                else:
                                    s_v.inc(v.scalar_tensor_tensor(
                                        ycmb[:, m * TC:(m + 1) * TC],
                                        yhalf[:, m * TC:(m + 1) * TC],
                                        cwc(wcol), ycmb[:, m * TC:(m + 1) * TC],
                                        ALU.mult, ALU.add))
                        if c == 0 and h == 0:
                            yh_used = [0]
                        yh_used[0] = s_v.n
                    v_y2 = s_v.n

                    @blk.sync
                    def _(q, cs=cs, v_y2=v_y2):
                        q.wait_ge(s_v.h, v_y2)
                        sd[9].inc(q.dma_start(
                            out=cc2_in[:, cs:cs + TC].rearrange("(m p) t -> p m t",
                                                                p=P),
                            in_=ycmb[:, :].rearrange("p (m t) -> p m t", t=TC)), 16)
                cc2_sp = sd[9].n

                @blk.gpsimd
                def _(g):
                    g.wait_ge(sd[9].h, cc2_sp)
                    s_cc.inc(g.collective_compute(
                        "AllGather", ALU.bypass,
                        replica_groups=[[0, 1, 2, 3], [4, 5, 6, 7]],
                        ins=[cc2_in.tensor.ap().opt()],
                        outs=[cc2_out.tensor.ap().opt()]))
            les.close()

        if PH >= 2:
            p2co = ExitStack()
            wo_g = p2co.enter_context(
                nc.sbuf_tensor("wo", [P, DIT * DT * P], BF16))
            dcol_g = p2co.enter_context(nc.sbuf_tensor("dcol", [P, DIT], F32))

            @blk.gpsimd
            def _(g):
                gd[5].inc(g.dma_start(
                    out=wo_g[:, :].rearrange("p (k m c) -> p k m c",
                                             m=DT, c=P),
                    in_=w_out.rearrange("(k p) (m c) -> p k m c", p=P, c=P)),
                    16)
                gd[5].inc(g.dma_start(out=dcol_g[:, :], in_=d_param), 16)
            ld_wo_g = gd[5].n

            _cc2()
            cc2_done_n = s_cc.n

        def _phase2c():
            les = ExitStack()
            def lsb(nm, shape, dt=F32):
                return les.enter_context(nc.sbuf_tensor(nm, shape, dt))
            wo = wo_g
            dcol = dcol_g
            ld_wo = ld_wo_g
            yfull = lsb("yfull", [P, DIT * TC], F32)
            gtile = lsb("gtile", [P, DIT * TC], BF16)
            cstg = lsb("cstg", [P, DHT * TC], BF16)
            cstg16 = lsb("cstg16", [P, DHT * TC], BF16)
            ustg = lsb("ustg", [P, DIT * TC], BF16)
            srest = lsb("srest", [P, DIT * TC], BF16)
            x2t = lsb("x2t", [P, DT * TC], F32)
            if True:
                for c in range(NOCH):
                    cs = c * TC
                    for h in range(2):
                        ld0 = gd[6].n

                        @blk.gpsimd
                        def _(g, cs=cs, h=h, c=c):
                            if c + h > 0:
                                g.wait_ge(s_v.h, cstg_used[0])
                            gd[6].inc(g.dma_start(
                                out=cstg[:, :].rearrange("p (m t) -> p m t", t=TC),
                                in_=y_all.rearrange("(m p) t -> p m t", p=P)[
                                    :, :, h * TO + cs:h * TO + cs + TC]), 16)
                        ld1 = gd[6].n

                        @blk.vector
                        def _(v, h=h, ld1=ld1):
                            v.wait_ge(gd[6].h, ld1)
                            wcol = CW_OWN + h
                            for m in range(0, DHT, 2):
                                vw(v)
                                if h == 0:
                                    s_v.inc(v.tensor_scalar(
                                        yfull[:, m * TC:(m + 2) * TC],
                                        cstg[:, m * TC:(m + 2) * TC],
                                        cwc(wcol), None, ALU.mult))
                                else:
                                    s_v.inc(v.scalar_tensor_tensor(
                                        yfull[:, m * TC:(m + 2) * TC],
                                        cstg[:, m * TC:(m + 2) * TC],
                                        cwc(wcol), yfull[:, m * TC:(m + 2) * TC],
                                        ALU.mult, ALU.add))
                        if c == 0 and h == 0:
                            cstg_used = [0]
                        cstg_used[0] = s_v.n
                    yown_done = s_v.n

                    for h in range(2):
                        @blk.gpsimd
                        def _(g, cs=cs, h=h, c=c):
                            if c + h > 0:
                                g.wait_ge(s_v.h, ustg_used[0])
                            gd[7].inc(g.dma_start(
                                out=ustg[:, :].rearrange("p (m t) -> p m t", t=TC),
                                in_=u_all.rearrange("(m p) t -> p m t", p=P)[
                                    :, :, h * TO + cs:h * TO + cs + TC]), 16)
                        ld1 = gd[7].n

                        @blk.vector
                        def _(v, h=h, ld1=ld1, c=c):
                            v.wait_ge(gd[7].h, ld1)
                            if c >= 1 and h == 0:
                                v.wait_ge(s_t.h, wo_mm_mark[0])
                            wcol = CW_OWN + h
                            for m in range(0, DIT, 2):
                                vw(v)
                                if h == 0:
                                    s_v.inc(v.tensor_scalar(
                                        gtile[:, m * TC:(m + 2) * TC],
                                        ustg[:, m * TC:(m + 2) * TC],
                                        cwc(wcol), None, ALU.mult))
                                else:
                                    s_v.inc(v.scalar_tensor_tensor(
                                        gtile[:, m * TC:(m + 2) * TC],
                                        ustg[:, m * TC:(m + 2) * TC],
                                        cwc(wcol), gtile[:, m * TC:(m + 2) * TC],
                                        ALU.mult, ALU.add))
                        if c == 0 and h == 0:
                            ustg_used = [0]
                        ustg_used[0] = s_v.n
                    u_done = s_v.n

                    ld_sx0 = sd[10].n

                    @blk.sync
                    def _(q, cs=cs, c=c, u_done=u_done):
                        if c >= 1:
                            q.wait_ge(s_v.h, g_done_mark[0])
                            q.wait_ge(sd[11].h, 16 * c)
                        sd[10].inc(q.dma_start(
                            out=srest[:, :].rearrange("p (m t) -> p m t", t=TC),
                            in_=sres_d.rearrange("(m p) t -> p m t", p=P)[
                                :, :, cs:cs + TC]), 16)
                        sd[10].inc(q.dma_start(
                            out=x2t[:, :].rearrange("p (d t) -> p d t", t=TC),
                            in_=x1sp.rearrange("(d p) t -> p d t", p=P)[
                                :, :, cs:cs + TC]), 16)
                    ld_sx = sd[10].n

                    for s in range(4):
                        @blk.gpsimd
                        def _(g, cs=cs, s=s, c=c):
                            if c == 0 and s == 0:
                                g.wait_ge(s_cc.h, cc2_done_n)
                            if c + s > 0:
                                g.wait_ge(s_v.h, pstg_used[0])
                            gd[6].inc(g.dma_start(
                                out=cstg16[:, :].rearrange("p (m t) -> p m t",
                                                           t=TC),
                                in_=cc2_out.rearrange("(s m p) t -> p s m t",
                                                      p=P, m=DHT)[
                                    :, s, :, cs:cs + TC]), 16)
                        ld1 = gd[6].n

                        @blk.vector
                        def _(v, s=s, ld1=ld1):
                            v.wait_ge(gd[6].h, ld1)
                            for m in range(0, DHT, 2):
                                vw(v)
                                if s == 0:
                                    s_v.inc(v.tensor_scalar(
                                        yfull[:, (DHT + m) * TC:(DHT + m + 2) * TC],
                                        cstg16[:, m * TC:(m + 2) * TC],
                                        cwc(CW_PARTNER + s), None, ALU.mult))
                                else:
                                    s_v.inc(v.scalar_tensor_tensor(
                                        yfull[:, (DHT + m) * TC:(DHT + m + 2) * TC],
                                        cstg16[:, m * TC:(m + 2) * TC],
                                        cwc(CW_PARTNER + s),
                                        yfull[:, (DHT + m) * TC:(DHT + m + 2) * TC],
                                        ALU.mult, ALU.add))
                        if c == 0 and s == 0:
                            pstg_used = [0]
                        pstg_used[0] = s_v.n
                    y_done = s_v.n

                    @blk.vector
                    def _(v, u_done=u_done, ld_sx=ld_sx, y_done=y_done):
                        v.wait_ge(s_v.h, u_done)
                        v.wait_ge(sd[10].h, ld_sx)
                        v.wait_ge(gd[5].h, ld_wo)
                        for m in range(DIT):
                            vw(v)
                            s_v.inc(v.scalar_tensor_tensor(
                                scrA[:, 0:TC], gtile[:, m * TC:(m + 1) * TC],
                                dcol[:, m:m + 1], yfull[:, m * TC:(m + 1) * TC],
                                ALU.mult, ALU.add))
                            vw(v)
                            s_v.inc(v.tensor_tensor(
                                gtile[:, m * TC:(m + 1) * TC], scrA[:, 0:TC],
                                srest[:, m * TC:(m + 1) * TC], ALU.mult))
                    if c == 0:
                        g_done_mark = [0]
                    g_done_mark[0] = s_v.n
                    g_done = s_v.n

                    def evac_wo(sc, m, bank):
                        raise RuntimeError

                    for m in range(DT):
                        bank = pA if m % 2 == 0 else pB
                        t_o0 = s_t.n

                        @blk.tensor
                        def _(t, m=m, bank=bank, g_done=g_done, c=c):
                            t.wait_ge(s_v.h, g_done)
                            t.wait_ge(gd[5].h, ld_wo)
                            if m >= 2:
                                t.wait_ge(s_v.h, wo_ev_mark[m - 2])
                            elif c >= 1 and m <= 1:
                                t.wait_ge(s_v.h, wo_ev_mark[DT - 2 + m])
                            for k in range(DIT):
                                s_t.inc(t.matmul(
                                    bank[:, :],
                                    wo[:, (k * DT + m) * P:(k * DT + m + 1) * P],
                                    gtile[:, k * TC:(k + 1) * TC],
                                    start=(k == 0), stop=(k == DIT - 1)))
                        t_o1 = s_t.n

                        @blk.vector
                        def _(v, m=m, bank=bank, t_o1=t_o1):
                            v.wait_ge(s_t.h, t_o1)
                            vw(v)
                            s_v.inc(v.tensor_tensor(x2t[:, m * TC:(m + 1) * TC],
                                                    x2t[:, m * TC:(m + 1) * TC],
                                                    bank[:, :], ALU.add))
                        if c == 0 and m == 0:
                            wo_ev_mark = {}
                            wo_mm_mark = [0]
                        wo_ev_mark[m] = s_v.n
                    wo_mm_mark[0] = s_t.n
                    x2_done = s_v.n

                    @blk.sync
                    def _(q, cs=cs, x2_done=x2_done):
                        q.wait_ge(s_v.h, x2_done)
                        sd[11].inc(q.dma_start(
                            out=x2sp[:, cs:cs + TC].rearrange("(d p) t -> p d t",
                                                              p=P),
                            in_=x2t[:, :].rearrange("p (d t) -> p d t", t=TC)), 16)
                x2_sp = sd[11].n

                if DBG >= 1:
                    @blk.gpsimd
                    def _(g):
                        g.wait_ge(sd[11].h, x2_sp)
                        gd[1].inc(g.dma_start(out=dbgo["dbg_x2"], in_=x2sp), 16)

                phase_barrier()
            les.close()

        if PH >= 2:
            _phase2c()
            p2co.close()

        BMAG = 0.0 if FP64_MODE else MAGIC
        PWL_A0F = float(np.float32(PWL_A0))
        PWL_KF = [(float(np.float32(t)), float(np.float32(d)))
                  for t, d in PWL_KNOTS]

        def _ffn(src_sp, dst_sp, final, sx, so, gw, tag):
            les = ExitStack()

            def lsb(nm, shape, dt=F32):
                return les.enter_context(nc.sbuf_tensor(nm, shape, dt))

            nonlocal rowA
            xn16 = lsb(tag + "xn16", [P, DT * TC], BF16)
            sq16 = lsb(tag + "sq16", [P, DT * TC], BF16)
            rowA = lsb(tag + "rowA", [1, 8 * TC], F32)
            QW = 16 * 4 * P
            w1b = lsb(tag + "w1", [P, DT * FFT * P], BF16)
            w2h = lsb(tag + "w2", [P, 2 * QW], BF16)
            h16 = lsb(tag + "h16", [P, FFT * TC], BF16)
            xc = lsb(tag + "xc", [P, DT * TC], F32)
            bq = lsb(tag + "bq", [P, FFT + DT], F32)
            ob = lsb(tag + "ob", [P, DT * P], F32)

            @blk.gpsimd
            def _(g):
                gd[gw].inc(g.dma_start(
                    out=w1b[:, :].rearrange("p (k m c) -> p k m c", m=FFT, c=P),
                    in_=ffn_w1.rearrange("(k p) (m c) -> p k m c", p=P, c=P)), 16)
                gd[gw].inc(g.dma_start(out=bq[:, 0:FFT], in_=ffn_b1), 16)
                gd[gw].inc(g.dma_start(out=bq[:, FFT:FFT + DT], in_=ffn_b2), 16)
            ld_w1 = gd[gw].n

            @blk.vector
            def _(v):
                v.wait_ge(gd[gw].h, ld_w1)
                vw(v)
                s_v.inc(v.tensor_scalar(bq[:, 0:FFT], bq[:, 0:FFT], 256.0, BMAG,
                                        ALU.mult, ALU.add))
                vw(v)
                s_v.inc(v.tensor_scalar(bq[:, FFT:FFT + DT], bq[:, FFT:FFT + DT],
                                        256.0, BMAG, ALU.mult, ALU.add))
            bq_done = s_v.n

            h_done = {}
            q_ev = {}
            ob_ev = {}
            xst_mark = [sd[so].n]
            xfree_t = [0]
            w2q_n = [0]
            w2q_rd = {}
            mm2_last = [0]
            ob_st = [sd[so].n]
            xn_rd_prev = [s_t.n]

            for c in range(NOCH):
                cs = c * TC

                @blk.sync
                def _(q, c=c, cs=cs):
                    if c >= 1:
                        if final:
                            q.wait_ge(s_t.h, xfree_t[0])
                        else:
                            q.wait_ge(sd[so].h, xst_mark[0])
                    sd[sx].inc(q.dma_start(
                        out=xc[:, :].rearrange("p (d t) -> p d t", t=TC),
                        in_=src_sp.rearrange("(d p) t -> p d t", p=P)[
                            :, :, cs:cs + TC]), 16)
                ld_x = sd[sx].n

                prev_xn = xn_rd_prev[0]
                vxn = emit_ln(lambda d: d * TC, xc, xn16, sq16,
                              pre_wait=(lambda e, ld_x=ld_x, prev_xn=prev_xn:
                                        (e.wait_ge(sd[sx].h, ld_x),
                                         e.wait_ge(s_t.h, prev_xn))))

                for m in range(FFT):
                    bank = [pA, pB, pC, pD][m % 4]

                    @blk.tensor
                    def _(t, m=m, bank=bank, vxn=vxn, c=c):
                        if m == 0:
                            t.wait_ge(s_v.h, vxn)
                            t.wait_ge(gd[gw].h, ld_w1)
                        if m >= 4:
                            t.wait_ge(s_v.h, h_done[m - 4])
                        elif c >= 1:
                            t.wait_ge(s_v.h, h_done[FFT - 4 + m])
                            if m in (2, 3):
                                t.wait_ge(s_v.h, q_ev[m + 2])
                        for k in range(DT):
                            s_t.inc(t.matmul(
                                bank[:, :],
                                w1b[:, (k * FFT + m) * P:(k * FFT + m + 1) * P],
                                xn16[:, k * TC:(k + 1) * TC],
                                start=(k == 0), stop=(k == DT - 1)))
                    t_m1 = s_t.n

                    @blk.vector
                    def _(v, m=m, bank=bank, t_m1=t_m1, c=c):
                        v.wait_ge(s_t.h, t_m1)
                        if m == 0:
                            if c == 0:
                                v.wait_ge(s_v.h, bq_done)
                            else:
                                v.wait_ge(s_t.h, mm2_last[0])
                        vw(v)
                        s_v.inc(v.tensor_scalar(scrA[:, 0:TC], bank[:, :], 256.0,
                                                bq[:, m:m + 1], ALU.mult, ALU.add))
                    v_t1 = s_v.n

                    if not FP64_MODE:
                        @blk.scalar
                        def _(sc, v_t1=v_t1):
                            sc.wait_ge(s_v.h, v_t1)
                            aw(sc)
                            s_a.inc(sc.activation(scrB[:, 0:TC], scrA[:, 0:TC],
                                                  AF.Gelu, scale=C(1.0 / 256.0),
                                                  bias=C(-MAGIC / 256.0)))
                        a_g = s_a.n

                        @blk.vector
                        def _(v, m=m, a_g=a_g):
                            v.wait_ge(s_a.h, a_g)
                            vw(v)
                            s_v.inc(v.tensor_scalar(scrA[:, 0:TC], scrB[:, 0:TC],
                                                    256.0, MAGIC, ALU.mult,
                                                    ALU.add))
                            vw(v)
                            s_v.inc(v.tensor_scalar(h16[:, m * TC:(m + 1) * TC],
                                                    scrA[:, 0:TC], MAGIC,
                                                    1.0 / 256.0, ALU.subtract,
                                                    ALU.mult))
                        h_done[m] = s_v.n
                    else:
                        @blk.vector
                        def _(v, m=m):
                            emit_rne_f(v, scrA[:, 0:TC], scrA[:, 0:TC], TC)
                            vw(v)
                            s_v.inc(v.tensor_scalar(scrD[:, 0:TC], scrA[:, 0:TC],
                                                    0.0, PWL_A0F, ALU.mult,
                                                    ALU.add))
                            for tj, dj in PWL_KF:
                                vw(v)
                                s_v.inc(v.tensor_scalar(scrC[:, 0:TC],
                                                        scrA[:, 0:TC], tj, 0.0,
                                                        ALU.subtract, ALU.max))
                                vw(v)
                                s_v.inc(v.scalar_tensor_tensor(
                                    scrD[:, 0:TC], scrC[:, 0:TC], dj,
                                    scrD[:, 0:TC], ALU.mult, ALU.add))
                            emit_rne_f(v, h16[:, m * TC:(m + 1) * TC],
                                       scrD[:, 0:TC], TC, post_mul=1.0 / 256.0)
                        h_done[m] = s_v.n
                xn_rd_prev[0] = s_t.n

                m2banks = [pC, pD, pE, pF]
                for grp in range(2):
                    mmdone = {}
                    for hf in range(2):
                        slot = w2q_n[0] % 2
                        myq = w2q_n[0]
                        w2q_n[0] += 1
                        qsem = gd[13 + slot]

                        @blk.gpsimd
                        def _(g, grp=grp, hf=hf, slot=slot, myq=myq):
                            g.wait_ge(gd[gw].h, ld_w1)
                            if myq >= 2:
                                g.wait_ge(s_t.h, w2q_rd[myq - 2])
                            qsem.inc(g.dma_start(
                                out=w2h[:, slot * QW:(slot + 1) * QW].rearrange(
                                    "p (k m c) -> p k m c", m=4, c=P),
                                in_=ffn_w2.rearrange("(k p) (m c) -> p k m c",
                                                     p=P, c=P)[
                                    :, hf * 16:(hf + 1) * 16,
                                    grp * 4:(grp + 1) * 4]), 16)
                        ldq = qsem.n

                        for mi in range(4):
                            m = grp * 4 + mi
                            bank = m2banks[mi]

                            @blk.tensor
                            def _(t, mi=mi, m=m, hf=hf, slot=slot, bank=bank,
                                  ldq=ldq, c=c, grp=grp, qsem=qsem):
                                if mi == 0:
                                    t.wait_ge(qsem.h, ldq)
                                    if hf == 0:
                                        t.wait_ge(s_v.h, h_done[FFT - 1])
                                if hf == 0:
                                    if grp == 1:
                                        t.wait_ge(s_v.h, q_ev[m - 4])
                                    elif c >= 1:
                                        t.wait_ge(s_v.h, q_ev[m + 4])
                                        if final and mi == 0:
                                            t.wait_ge(s_v.h,
                                                      ob_ev[4 * DT - 1])
                                for k in range(16):
                                    kk = hf * 16 + k
                                    s_t.inc(t.matmul(
                                        bank[:, :],
                                        w2h[:, slot * QW + (k * 4 + mi) * P:
                                            slot * QW + (k * 4 + mi + 1) * P],
                                        h16[:, kk * TC:(kk + 1) * TC],
                                        start=(hf == 0 and k == 0),
                                        stop=(hf == 1 and k == 15)))
                            if hf == 1:
                                mmdone[mi] = s_t.n
                        w2q_rd[myq] = s_t.n

                    for mi in range(4):
                        m = grp * 4 + mi
                        bank = m2banks[mi]

                        @blk.vector
                        def _(v, m=m, bank=bank, e=mmdone[mi], ld_x=ld_x,
                              grp=grp, mi=mi):
                            v.wait_ge(s_t.h, e)
                            if grp == 0 and mi == 0:
                                v.wait_ge(sd[sx].h, ld_x)
                            vw(v)
                            s_v.inc(v.tensor_scalar(
                                scrA[:, 0:TC], bank[:, :], 256.0,
                                bq[:, FFT + m:FFT + m + 1],
                                ALU.mult, ALU.add))
                            if not FP64_MODE:
                                vw(v)
                                s_v.inc(v.tensor_scalar(
                                    scrB[:, 0:TC], scrA[:, 0:TC], MAGIC,
                                    1.0 / 256.0, ALU.subtract, ALU.mult))
                            else:
                                emit_rne_f(v, scrB[:, 0:TC], scrA[:, 0:TC],
                                           TC, post_mul=1.0 / 256.0)
                            vw(v)
                            s_v.inc(v.tensor_tensor(
                                xc[:, m * TC:(m + 1) * TC],
                                xc[:, m * TC:(m + 1) * TC],
                                scrB[:, 0:TC], ALU.add))
                        q_ev[m] = s_v.n
                mm2_last[0] = s_t.n

                if not final:
                    @blk.sync
                    def _(q, cs=cs, e=q_ev[DT - 1]):
                        q.wait_ge(s_v.h, e)
                        sd[so].inc(q.dma_start(
                            out=dst_sp[:, cs:cs + TC].rearrange(
                                "(d p) t -> p d t", p=P),
                            in_=xc[:, :].rearrange("p (d t) -> p d t", t=TC)), 16)
                    xst_mark[0] = sd[so].n
                else:
                    x5v = q_ev[DT - 1]
                    for tb in range(TC // P):
                        for a in range(DT):
                            j = tb * DT + a
                            bank = pE if j % 2 == 0 else pF

                            @blk.tensor
                            def _(t, a=a, tb=tb, j=j, bank=bank, x5v=x5v, c=c):
                                if j == 0:
                                    t.wait_ge(s_v.h, x5v)
                                if j >= 2:
                                    t.wait_ge(s_v.h, ob_ev[j - 2])
                                elif c >= 1:
                                    t.wait_ge(s_v.h,
                                              ob_ev[4 * DT - 2 + j])
                                s_t.inc(t.transpose(
                                    bank[:, 0:P],
                                    xc[:, a * TC + tb * P:a * TC + (tb + 1) * P],
                                    idf[:, :]))
                            trm = s_t.n

                            @blk.vector
                            def _(v, a=a, bank=bank, trm=trm):
                                v.wait_ge(s_t.h, trm)
                                if a == 0 and ob_st[0]:
                                    v.wait_ge(sd[so].h, ob_st[0])
                                vw(v)
                                s_v.inc(v.tensor_copy(ob[:, a * P:(a + 1) * P],
                                                      bank[:, 0:P]))
                            ob_ev[j] = s_v.n

                        @blk.sync
                        def _(q, cs=cs, tb=tb, e=ob_ev[tb * DT + DT - 1]):
                            q.wait_ge(s_v.h, e)
                            sd[so].inc(q.dma_start(
                                out=x_out[cs + tb * P:cs + (tb + 1) * P, :],
                                in_=ob[:, :]), 16)
                        ob_st[0] = sd[so].n
                    xfree_t[0] = s_t.n

            phase_barrier()
            les.close()

        if PH >= 3:
            _ffn(x2sp, x3sp, False, 12, 13, 8, "f3")
            if DBG >= 1:
                @blk.gpsimd
                def _(g):
                    gd[1].inc(g.dma_start(out=dbgo["dbg_x3"], in_=x3sp), 16)
                phase_barrier()

        def _phase4():
            les = ExitStack()

            def lsb(nm, shape, dt=F32):
                return les.enter_context(nc.sbuf_tensor(nm, shape, dt))

            nonlocal rowA
            xn16 = lsb("p4xn16", [P, DT * TC], BF16)
            sq16 = lsb("p4sq16", [P, DT * TC], BF16)
            rowA = lsb("p4rowA", [1, 8 * TC], F32)
            k_own = lsb("k_own", [P, DT * TO], BF16)
            vT_own = lsb("vT_own", [P, 16 * D], BF16)
            k_rect = k_own
            vT_rect = vT_own
            wq_sb = lsb("wq_sb", [P, 3 * DT * P], BF16)
            ropeb = lsb("ropeb", [P, 2 * TC])
            rst = lsb("rst", [P, DT * TC], BF16)
            x3c = lsb("x3c", [P, DT * TC], F32)
            qblk = lsb("qblk", [P, DT * P], BF16)
            Ef = lsb("Ef", [P, 4 * TC], F32)
            ETb = lsb("ETb", [P, 4 * TC], BF16)
            aO = lsb("aO", [P, D + 2], F32)
            stc = lsb("stc", [P, 16], F32)

            @blk.gpsimd
            def _(g):
                gd[9].inc(g.dma_start(
                    out=wq_sb[:, :].rearrange("p (i c) -> p i c", c=P),
                    in_=wqkv.rearrange("(i p) c -> p i c", p=P)), 16)
            ld_wq = gd[9].n

            pj = {"n": 0, "ev": {}}
            rope_rd = [0]
            qcast = [gd[11].n]
            vtr_ev = {"n": 0, "ev": {}}
            scrC_rd = [0]

            def emit_qkv(cs, slist, cosap, sinap, vxn, kdst, vtdst, qdram, tb0,
                         first_t_wait=None):
                for a in range(DT):
                    @blk.sync
                    def _(q, a=a):
                        if rope_rd[0]:
                            q.wait_ge(s_v.h, rope_rd[0])
                        sd[16].inc(q.dma_start(out=ropeb[0:64, 0:TC],
                                               in_=cosap(a)), 16)
                        sd[16].inc(q.dma_start(out=ropeb[0:64, TC:2 * TC],
                                               in_=sinap(a)), 16)
                    ldr = sd[16].n

                    for s in slist:
                        si = {"q": 0, "k": 1, "v": 2}[s]
                        bank = pA if pj["n"] % 2 == 0 else pB
                        pj["n"] += 1
                        myn = pj["n"]

                        @blk.tensor
                        def _(t, a=a, si=si, bank=bank, myn=myn, vxn=vxn):
                            if myn == 1:
                                t.wait_ge(gd[9].h, ld_wq)
                            if first_t_wait is not None and a == 0:
                                first_t_wait(t)
                            t.wait_ge(s_v.h, vxn)
                            if myn - 2 >= 1:
                                t.wait_ge(s_v.h, pj["ev"][myn - 2])
                            s_t.inc(t.matmul(
                                bank[:, :],
                                wq_sb[:, (si * DT + a) * P:(si * DT + a + 1) * P],
                                xn16[:, a * TC:(a + 1) * TC],
                                start=True, stop=True))
                        t_pj = s_t.n

                        if s in ("q", "k"):
                            @blk.vector
                            def _(v, a=a, s=s, bank=bank, t_pj=t_pj, ldr=ldr):
                                v.wait_ge(s_t.h, t_pj)
                                v.wait_ge(sd[16].h, ldr)
                                if s == "q" and qcast[0]:
                                    v.wait_ge(gd[11].h, qcast[0])
                                if s == "k":
                                    d1 = kdst[0:64, a * TO + cs:a * TO + cs + TC]
                                    d2 = kdst[64:128, a * TO + cs:a * TO + cs + TC]
                                else:
                                    d1 = scrC[0:64, 0:TC]
                                    d2 = scrC[64:128, 0:TC]
                                cosr = ropeb[0:64, 0:TC]
                                sinr = ropeb[0:64, TC:2 * TC]
                                vw(v)
                                s_v.inc(v.tensor_tensor(scrA[0:64, 0:TC],
                                                        bank[0:64, :], cosr,
                                                        ALU.mult))
                                vw(v)
                                s_v.inc(v.tensor_tensor(scrB[0:64, 0:TC],
                                                        bank[64:128, :], sinr,
                                                        ALU.mult))
                                vw(v)
                                s_v.inc(v.tensor_tensor(d1, scrA[0:64, 0:TC],
                                                        scrB[0:64, 0:TC],
                                                        ALU.subtract))
                                vw(v)
                                s_v.inc(v.tensor_tensor(scrA[0:64, 0:TC],
                                                        bank[0:64, :], sinr,
                                                        ALU.mult))
                                vw(v)
                                s_v.inc(v.tensor_tensor(scrB[0:64, 0:TC],
                                                        bank[64:128, :], cosr,
                                                        ALU.mult))
                                vw(v)
                                s_v.inc(v.tensor_tensor(d2, scrA[0:64, 0:TC],
                                                        scrB[0:64, 0:TC],
                                                        ALU.add))
                            pj["ev"][myn] = s_v.n
                            rope_rd[0] = s_v.n
                            if s == "q":
                                @blk.gpsimd
                                def _(g, a=a, e=s_v.n):
                                    g.wait_ge(s_v.h, e)
                                    gd[11].inc(g.dma_start(
                                        out=qdram[a * P:(a + 1) * P, cs:cs + TC],
                                        in_=scrC[:, 0:TC]), 16)
                                qcast[0] = gd[11].n
                        else:
                            @blk.vector
                            def _(v, bank=bank, t_pj=t_pj):
                                v.wait_ge(s_t.h, t_pj)
                                if vtr_last_t[0]:
                                    v.wait_ge(s_t.h, vtr_last_t[0])
                                vw(v)
                                s_v.inc(v.tensor_copy(scrD[:, 0:TC], bank[:, :]))
                            pj["ev"][myn] = s_v.n
                            v_ev = s_v.n

                            for tsub in range(TC // P):
                                tbank = pE if vtr_ev["n"] % 2 == 0 else pF
                                vtr_ev["n"] += 1
                                mytr = vtr_ev["n"]

                                @blk.tensor
                                def _(t, tsub=tsub, tbank=tbank, v_ev=v_ev,
                                      mytr=mytr):
                                    t.wait_ge(s_v.h, v_ev)
                                    if mytr - 2 >= 1:
                                        t.wait_ge(s_v.h, vtr_ev["ev"][mytr - 2])
                                    s_t.inc(t.transpose(
                                        tbank[:, 0:P],
                                        scrD[:, tsub * P:(tsub + 1) * P],
                                        idf[:, :]))
                                t_tr = s_t.n

                                @blk.vector
                                def _(v, a=a, tsub=tsub, tbank=tbank, t_tr=t_tr,
                                      tb0=tb0):
                                    v.wait_ge(s_t.h, t_tr)
                                    vw(v)
                                    s_v.inc(v.tensor_copy(
                                        vtdst[:, (tb0 + tsub) * D + a * P:
                                              (tb0 + tsub) * D + (a + 1) * P],
                                        tbank[:, 0:P]))
                                vtr_ev["ev"][mytr] = s_v.n
                            vtr_last_t[0] = s_t.n

            vtr_last_t = [0]
            x3rd_v = [0]
            x3rd_a = [0]
            xn_rd_t = [s_t.n]
            cc3_mark = [sd[15].n]

            for c in range(NOCH):
                cs = c * TC

                @blk.sync
                def _(q, c=c, cs=cs):
                    if c >= 1:
                        q.wait_ge(s_v.h, x3rd_v[0])
                        q.wait_ge(s_a.h, x3rd_a[0])
                    sd[14].inc(q.dma_start(
                        out=x3c[:, :].rearrange("p (d t) -> p d t", t=TC),
                        in_=x3sp.rearrange("(d p) t -> p d t", p=P)[
                            :, :, cs:cs + TC]), 16)
                ld_x = sd[14].n

                prev_xn = xn_rd_t[0]
                prev_cc = cc3_mark[0]
                vxn = emit_ln(lambda d: d * TC, x3c, xn16, sq16,
                              pre_wait=(lambda e, ld_x=ld_x, prev_xn=prev_xn,
                                        prev_cc=prev_cc:
                                        (e.wait_ge(sd[14].h, ld_x),
                                         e.wait_ge(s_t.h, prev_xn),
                                         e.wait_ge(sd[15].h, prev_cc)
                                         if prev_cc else None)))
                x3rd_v[0] = vxn
                x3rd_a[0] = s_a.n

                @blk.sync
                def _(q, cs=cs, vxn=vxn):
                    q.wait_ge(s_v.h, vxn)
                    sd[15].inc(q.dma_start(
                        out=cc3_in[:, cs:cs + TC].rearrange("(d p) t -> p d t",
                                                            p=P),
                        in_=xn16[:, :].rearrange("p (d t) -> p d t", t=TC)), 16)
                cc3_mark[0] = sd[15].n

                emit_qkv(cs, "qkv",
                         lambda a, cs=cs: rope_cos_own[a * 64:(a + 1) * 64,
                                                       cs:cs + TC],
                         lambda a, cs=cs: rope_sin_own[a * 64:(a + 1) * 64,
                                                       cs:cs + TC],
                         vxn, k_own, vT_own, qown_d, c * 4)
                xn_rd_t[0] = s_t.n
            cc3_sp = sd[15].n
            own_proj_v = s_v.n
            own_qcast = qcast[0]

            @blk.gpsimd
            def _(g):
                g.wait_ge(sd[15].h, cc3_sp)
                g.wait_ge(gd[11].h, qcast[0])
                s_cc.inc(g.collective_compute(
                    "AllGather", ALU.bypass,
                    replica_groups=[[0, 1, 2, 3], [4, 5, 6, 7]],
                    ins=[cc3_in.tensor.ap().opt()],
                    outs=[cc3_out.tensor.ap().opt()]))
                g.wait_ge(s_cc.h, s_cc.n)
            cc3_done = s_cc.n

            ab = {"exp_a": 0, "et_v": 0, "av_t": 0, "aO_v": 0,
                  "o_st": sd[18].n, "q_rd": 0, "first": True}

            def attn_block(i, nblocks, qdram, ksb, vtsb, odram, masked,
                           extra_q_wait=None, extra_s_wait=None):
                nstrips = (nblocks + 3) // 4
                sbanks = [pA, pB, pC, pD]

                @blk.sync
                def _(q, i=i):
                    if extra_q_wait is not None:
                        extra_q_wait(q)
                    if not ab["first"]:
                        q.wait_ge(s_t.h, ab["q_rd"])
                    sd[17].inc(q.dma_start(
                        out=qblk[:, :].rearrange("p (a c) -> p a c", c=P),
                        in_=qdram.rearrange("(a p) t -> p a t", p=P)[
                            :, :, i * P:(i + 1) * P]), 16)
                qld = sd[17].n

                strip_mm = []
                for s in range(nstrips):
                    ncols = min(TC, nblocks * P - s * TC)
                    bank = sbanks[s]

                    @blk.tensor
                    def _(t, s=s, ncols=ncols, bank=bank, qld=qld):
                        if s == 0:
                            t.wait_ge(sd[17].h, qld)
                            if extra_s_wait is not None:
                                extra_s_wait(t)
                            if not ab["first"]:
                                t.wait_ge(s_a.h, ab["exp_a"])
                                t.wait_ge(s_v.h, ab["et_v"])
                        for a in range(DT):
                            s_t.inc(t.matmul(
                                bank[:, 0:ncols],
                                qblk[:, a * P:(a + 1) * P],
                                ksb[:, a * TO + s * TC:a * TO + s * TC + ncols],
                                start=(a == 0), stop=(a == DT - 1)))
                    strip_mm.append(s_t.n)
                ab["q_rd"] = s_t.n

                if masked:
                    diag_s = (nblocks - 1) // 4
                    diag_off = ((nblocks - 1) % 4) * P
                    dbank = sbanks[diag_s]

                    @blk.vector
                    def _(v, e=strip_mm[diag_s], dbank=dbank, diag_off=diag_off):
                        v.wait_ge(s_t.h, e)
                        vw(v)
                        s_v.inc(v.tensor_tensor(
                            dbank[:, diag_off:diag_off + P],
                            dbank[:, diag_off:diag_off + P],
                            trim[:, :], ALU.add))
                    mask_v = s_v.n
                else:
                    mask_v = 0

                for s in range(nstrips):
                    ncols = min(TC, nblocks * P - s * TC)

                    @blk.vector
                    def _(v, s=s, ncols=ncols, e=strip_mm[s], mask_v=mask_v):
                        v.wait_ge(s_t.h, e)
                        if mask_v:
                            v.wait_ge(s_v.h, mask_v)
                        vw(v)
                        s_v.inc(v.tensor_reduce(stc[:, s:s + 1],
                                                sbanks[s][:, 0:ncols],
                                                AX.X, ALU.max))
                mc = 0 if nstrips == 1 else 4

                @blk.vector
                def _(v, nstrips=nstrips, mc=mc):
                    if nstrips > 1:
                        vw(v)
                        s_v.inc(v.tensor_tensor(stc[:, 4:5], stc[:, 0:1],
                                                stc[:, 1:2], ALU.max))
                        for s in range(2, nstrips):
                            vw(v)
                            s_v.inc(v.tensor_tensor(stc[:, 4:5], stc[:, 4:5],
                                                    stc[:, s:s + 1], ALU.max))
                    vw(v)
                    s_v.inc(v.tensor_scalar(stc[:, 5:6], stc[:, mc:mc + 1],
                                            -GAMMA, 0.0, ALU.mult, ALU.add))
                    vw(v)
                    s_v.inc(v.tensor_scalar(stc[:, 6:7], stc[:, mc:mc + 1],
                                            GAMMA, 0.0, ALU.mult, ALU.add))
                red_v = s_v.n

                for s in range(nstrips):
                    ncols = min(TC, nblocks * P - s * TC)

                    @blk.scalar
                    def _(sc, s=s, ncols=ncols, red_v=red_v):
                        sc.wait_ge(s_v.h, red_v)
                        if s == 0 and not ab["first"]:
                            sc.wait_ge(s_t.h, ab["av_t"])
                        aw(sc)
                        s_a.inc(sc.activation(
                            Ef[:, s * TC:s * TC + ncols],
                            sbanks[s][:, 0:ncols], AF.Exp,
                            scale=C(GAMMA), bias=stc[:, 5:6],
                            accum_out=stc[:, 8 + s:9 + s]))
                ab["exp_a"] = s_a.n
                exp_a = s_a.n

                scol = 8 if nstrips == 1 else 7

                @blk.vector
                def _(v, nstrips=nstrips, exp_a=exp_a):
                    v.wait_ge(s_a.h, exp_a)
                    if nstrips > 1:
                        vw(v)
                        s_v.inc(v.tensor_tensor(stc[:, 7:8], stc[:, 8:9],
                                                stc[:, 9:10], ALU.add))
                        for s in range(2, nstrips):
                            vw(v)
                            s_v.inc(v.tensor_tensor(stc[:, 7:8], stc[:, 7:8],
                                                    stc[:, 8 + s:9 + s],
                                                    ALU.add))
                sum_v = s_v.n

                et_ev = {}
                for b in range(nblocks):
                    tbank = pA if b % 2 == 0 else pB

                    @blk.tensor
                    def _(t, b=b, tbank=tbank, exp_a=exp_a):
                        if b == 0:
                            t.wait_ge(s_a.h, exp_a)
                        if b >= 2:
                            t.wait_ge(s_v.h, et_ev[b - 2])
                        s_t.inc(t.transpose(tbank[:, 0:P],
                                            Ef[:, b * P:(b + 1) * P],
                                            idf[:, :]))
                    trm = s_t.n

                    @blk.vector
                    def _(v, b=b, tbank=tbank, trm=trm):
                        v.wait_ge(s_t.h, trm)
                        vw(v)
                        s_v.inc(v.tensor_copy(ETb[:, b * P:(b + 1) * P],
                                              tbank[:, 0:P]))
                    et_ev[b] = s_v.n
                ab["et_v"] = s_v.n

                for hf in range(2):
                    obank = pE if hf == 0 else pF

                    @blk.tensor
                    def _(t, hf=hf, obank=obank, nblocks=nblocks):
                        for b in range(nblocks):
                            if hf == 0:
                                t.wait_ge(s_v.h, et_ev[b])
                            if b == 0 and not ab["first"]:
                                t.wait_ge(s_v.h, ab["aO_v"])
                            s_t.inc(t.matmul(
                                obank[:, 0:TC],
                                ETb[:, b * P:(b + 1) * P],
                                vtsb[:, b * D + hf * TC:b * D + (hf + 1) * TC],
                                start=(b == 0), stop=(b == nblocks - 1)))
                av_t = s_t.n
                ab["av_t"] = av_t

                @blk.vector
                def _(v, av_t=av_t, sum_v=sum_v, scol=scol):
                    v.wait_ge(s_t.h, av_t)
                    v.wait_ge(s_v.h, sum_v)
                    if ab["o_st"]:
                        v.wait_ge(sd[18].h, ab["o_st"])
                    vw(v)
                    s_v.inc(v.tensor_copy(aO[:, 0:TC], pE[:, 0:TC]))
                    vw(v)
                    s_v.inc(v.tensor_copy(aO[:, TC:2 * TC], pF[:, 0:TC]))
                    vw(v)
                    s_v.inc(v.tensor_copy(aO[:, D:D + 1], stc[:, 6:7]))
                    vw(v)
                    s_v.inc(v.tensor_copy(aO[:, D + 1:D + 2],
                                          stc[:, scol:scol + 1]))
                ab["aO_v"] = s_v.n

                @blk.sync
                def _(q, i=i, e=ab["aO_v"]):
                    q.wait_ge(s_v.h, e)
                    sd[18].inc(q.dma_start(out=odram[i * P:(i + 1) * P, :],
                                           in_=aO[:, :]), 16)
                ab["o_st"] = sd[18].n
                ab["first"] = False

            for i in range(TO // P):
                attn_block(
                    i, i + 1, qown_d, k_own, vT_own, atri, True,
                    extra_q_wait=(lambda q: q.wait_ge(gd[11].h, own_qcast))
                    if i == 0 else None,
                    extra_s_wait=(lambda t: t.wait_ge(s_v.h, own_proj_v))
                    if i == 0 else None)

            comb_used = [0]
            tri_end_v = s_v.n
            tri_end_a = s_a.n

            for half, slist in ((0, "kv"), (1, "q")):
                wsel = CW_SLAB_LO if half == 0 else CW_SLAB_HI
                for c in range(NOCH):
                    cs = c * TC
                    for s4 in range(4):
                        @blk.gpsimd
                        def _(g, s4=s4, cs=cs):
                            if comb_used[0]:
                                g.wait_ge(s_v.h, comb_used[0])
                            gd[10].inc(g.dma_start(
                                out=rst[:, :].rearrange("p (d t) -> p d t",
                                                        t=TC),
                                in_=cc3_out.rearrange("(s d p) t -> p s d t",
                                                      p=P, d=DT)[
                                    :, s4, :, cs:cs + TC]), 16)
                        ld1 = gd[10].n

                        @blk.vector
                        def _(v, s4=s4, ld1=ld1, wsel=wsel,
                              e=xn_rd_t[0]):
                            v.wait_ge(gd[10].h, ld1)
                            if s4 == 0:
                                v.wait_ge(s_t.h, e)
                            for d in range(0, DT, 2):
                                vw(v)
                                if s4 == 0:
                                    s_v.inc(v.tensor_scalar(
                                        xn16[:, d * TC:(d + 2) * TC],
                                        rst[:, d * TC:(d + 2) * TC],
                                        cwc(wsel + s4), None, ALU.mult))
                                else:
                                    s_v.inc(v.scalar_tensor_tensor(
                                        xn16[:, d * TC:(d + 2) * TC],
                                        rst[:, d * TC:(d + 2) * TC],
                                        cwc(wsel + s4),
                                        xn16[:, d * TC:(d + 2) * TC],
                                        ALU.mult, ALU.add))
                        comb_used[0] = s_v.n
                    vcomb = s_v.n

                    col0 = 0 if half == 0 else TO
                    ftw = ((lambda t: t.wait_ge(s_a.h, tri_end_a))
                           if (half == 0 and c == 0) else None)
                    emit_qkv(cs, slist,
                             lambda a, cs=cs, col0=col0:
                             rope_cos[a * 64:(a + 1) * 64,
                                      col0 + cs:col0 + cs + TC],
                             lambda a, cs=cs, col0=col0:
                             rope_sin[a * 64:(a + 1) * 64,
                                      col0 + cs:col0 + cs + TC],
                             vcomb, k_rect, vT_rect, qodd_d, c * 4,
                             first_t_wait=ftw)
                    xn_rd_t[0] = s_t.n
            rect_src_v = s_v.n
            rect_qcast = qcast[0]

            for i in range(TO // P):
                attn_block(
                    i, TO // P, qodd_d, k_rect, vT_rect, arect, False,
                    extra_q_wait=(lambda q: q.wait_ge(gd[11].h, rect_qcast))
                    if i == 0 else None,
                    extra_s_wait=(lambda t: t.wait_ge(s_v.h, rect_src_v))
                    if i == 0 else None)

            phase_barrier()
            les.close()

            mes = ExitStack()
            x3t = mes.enter_context(nc.sbuf_tensor("x3t", [P, DT * TC], F32))
            aO1 = mes.enter_context(nc.sbuf_tensor("aO1", [P, D + 2], F32))
            aO2 = mes.enter_context(nc.sbuf_tensor("aO2", [P, D + 2], F32))
            mrg = mes.enter_context(nc.sbuf_tensor("mrg", [P, D], F32))
            tr_add = {}
            x4_st = [sd[20].n]
            mrg_rd_t = [0]
            ao_used = [0]

            for c in range(NOCH):
                cs = c * TC

                @blk.sync
                def _(q, c=c, cs=cs):
                    if c >= 1:
                        q.wait_ge(sd[20].h, x4_st[0])
                    sd[19].inc(q.dma_start(
                        out=x3t[:, :].rearrange("p (d t) -> p d t", t=TC),
                        in_=x3sp.rearrange("(d p) t -> p d t", p=P)[
                            :, :, cs:cs + TC]), 16)
                ld_x3 = sd[19].n

                for j in range(4):
                    i = c * 4 + j

                    @blk.sync
                    def _(q, i=i):
                        if ao_used[0]:
                            q.wait_ge(s_v.h, ao_used[0])
                        sd[23].inc(q.dma_start(out=aO1[:, :],
                                               in_=atri[i * P:(i + 1) * P, :]),
                                   16)
                        sd[23].inc(q.dma_start(out=aO2[:, :],
                                               in_=arect[i * P:(i + 1) * P, :]),
                                   16)
                    ld_a = sd[23].n

                    @blk.vector
                    def _(v, ld_a=ld_a, e=mrg_rd_t[0]):
                        v.wait_ge(sd[23].h, ld_a)
                        if e:
                            v.wait_ge(s_t.h, e)
                        vw(v)
                        s_v.inc(v.tensor_scalar(scrA[:, 0:1], aO2[:, D:D + 1],
                                                cwc(CW_MERGE_W),
                                                cwc(CW_MERGE_B),
                                                ALU.mult, ALU.add))
                        vw(v)
                        s_v.inc(v.tensor_scalar(scrA[:, 1:2],
                                                aO2[:, D + 1:D + 2],
                                                cwc(CW_MERGE_W), None,
                                                ALU.mult))
                        vw(v)
                        s_v.inc(v.tensor_tensor(scrA[:, 2:3], aO1[:, D:D + 1],
                                                scrA[:, 0:1], ALU.max))
                        vw(v)
                        s_v.inc(v.tensor_tensor(scrA[:, 3:4], aO1[:, D:D + 1],
                                                scrA[:, 2:3], ALU.subtract))
                        vw(v)
                        s_v.inc(v.tensor_tensor(scrA[:, 4:5], scrA[:, 0:1],
                                                scrA[:, 2:3], ALU.subtract))
                    v_pre = s_v.n

                    @blk.scalar
                    def _(sc, v_pre=v_pre):
                        sc.wait_ge(s_v.h, v_pre)
                        aw(sc)
                        s_a.inc(sc.activation(scrA[:, 5:6], scrA[:, 3:4],
                                              AF.Exp))
                        aw(sc)
                        s_a.inc(sc.activation(scrA[:, 6:7], scrA[:, 4:5],
                                              AF.Exp))
                    a_e2 = s_a.n

                    @blk.vector
                    def _(v, a_e2=a_e2):
                        v.wait_ge(s_a.h, a_e2)
                        vw(v)
                        s_v.inc(v.tensor_tensor(scrA[:, 7:8],
                                                aO1[:, D + 1:D + 2],
                                                scrA[:, 5:6], ALU.mult))
                        vw(v)
                        s_v.inc(v.tensor_tensor(scrA[:, 0:1], scrA[:, 1:2],
                                                scrA[:, 6:7], ALU.mult))
                        vw(v)
                        s_v.inc(v.tensor_tensor(scrA[:, 7:8], scrA[:, 7:8],
                                                scrA[:, 0:1], ALU.add))
                    v_s = s_v.n

                    @blk.scalar
                    def _(sc, v_s=v_s):
                        sc.wait_ge(s_v.h, v_s)
                        aw(sc)
                        s_a.inc(sc.activation(scrA[:, 8:9], scrA[:, 7:8],
                                              AF.Ln))
                        aw(sc)
                        s_a.inc(sc.activation(scrA[:, 9:10], scrA[:, 8:9],
                                              AF.Exp, scale=C(-1.0)))
                    a_r = s_a.n

                    @blk.vector
                    def _(v, a_r=a_r):
                        v.wait_ge(s_a.h, a_r)
                        vw(v)
                        s_v.inc(v.tensor_scalar(mrg[:, :], aO1[:, 0:D],
                                                scrA[:, 5:6], None, ALU.mult))
                        vw(v)
                        s_v.inc(v.scalar_tensor_tensor(mrg[:, :], aO2[:, 0:D],
                                                       scrA[:, 6:7], mrg[:, :],
                                                       ALU.mult, ALU.add))
                        vw(v)
                        s_v.inc(v.tensor_scalar(mrg[:, :], mrg[:, :],
                                                scrA[:, 9:10], None, ALU.mult))
                    v_m = s_v.n
                    ao_used[0] = v_m

                    for a in range(DT):
                        jj = i * DT + a
                        bank = pE if jj % 2 == 0 else pF

                        @blk.tensor
                        def _(t, a=a, jj=jj, bank=bank, v_m=v_m):
                            if a == 0:
                                t.wait_ge(s_v.h, v_m)
                            if jj - 2 in tr_add:
                                t.wait_ge(s_v.h, tr_add[jj - 2])
                            s_t.inc(t.transpose(bank[:, 0:P],
                                                mrg[:, a * P:(a + 1) * P],
                                                idf[:, :]))
                        trm = s_t.n

                        @blk.vector
                        def _(v, a=a, j=j, bank=bank, trm=trm, ld_x3=ld_x3):
                            v.wait_ge(s_t.h, trm)
                            if a == 0:
                                v.wait_ge(sd[19].h, ld_x3)
                            vw(v)
                            s_v.inc(v.tensor_tensor(
                                x3t[:, a * TC + j * P:a * TC + (j + 1) * P],
                                x3t[:, a * TC + j * P:a * TC + (j + 1) * P],
                                bank[:, 0:P], ALU.add))
                        tr_add[jj] = s_v.n
                    mrg_rd_t[0] = s_t.n

                @blk.sync
                def _(q, cs=cs, e=tr_add[(c * 4 + 3) * DT + DT - 1]):
                    q.wait_ge(s_v.h, e)
                    sd[20].inc(q.dma_start(
                        out=x4sp[:, cs:cs + TC].rearrange("(d p) t -> p d t",
                                                          p=P),
                        in_=x3t[:, :].rearrange("p (d t) -> p d t", t=TC)), 16)
                x4_st[0] = sd[20].n

            phase_barrier()
            mes.close()

        if PH >= 4:
            _phase4()
            if DBG >= 1:
                @blk.gpsimd
                def _(g):
                    gd[1].inc(g.dma_start(out=dbgo["dbg_x4"], in_=x4sp), 16)
                phase_barrier()

        if PH >= 5:
            _ffn(x4sp, None, True, 21, 22, 12, "f5")

        @blk.sync
        def _(q):
            for s in sd:
                q.wait_ge(s.h, s.n)
            q.wait_ge(s_v.h, s_v.n)
            q.wait_ge(s_a.h, s_a.n)
            q.wait_ge(s_t.h, s_t.n)
            q.wait_ge(s_p.h, s_p.n)
            q.wait_ge(s_cc.h, s_cc.n)
            if PH < 5:
                sd[23].inc(q.dma_start(out=x_out, in_=x_in), 16)
                q.wait_ge(sd[23].h, sd[23].n)

        @blk.gpsimd
        def _(g):
            for s in gd:
                g.wait_ge(s.h, s.n)

    return nc


def _stage(inputs):
    x = np.asarray(inputs["L0_attn_a_x"], np.float32)
    W_in = np.asarray(inputs["L0_attn_a_W_in"], np.float32)
    W_delta = np.asarray(inputs["L0_attn_a_W_delta"], np.float32)
    b_delta = np.asarray(inputs["L0_attn_a_b_delta"], np.float32)
    W_B = np.asarray(inputs["L0_attn_a_W_B"], np.float32)
    W_C = np.asarray(inputs["L0_attn_a_W_C"], np.float32)
    A_log = np.asarray(inputs["L0_attn_a_A_log"], np.float32)
    D_param = np.asarray(inputs["L0_attn_a_D_param"], np.float32)
    W_out = np.asarray(inputs["L0_attn_a_W_out"], np.float32)
    w_q = np.asarray(inputs["L0_attn_b_w_q"], np.float32)
    w_k = np.asarray(inputs["L0_attn_b_w_k"], np.float32)
    w_v = np.asarray(inputs["L0_attn_b_w_v"], np.float32)
    w1 = np.asarray(inputs["L0_ffn_w1"], np.float32)
    b1 = np.asarray(inputs["L0_ffn_b1"], np.float32)
    w2 = np.asarray(inputs["L0_ffn_w2"], np.float32)
    b2 = np.asarray(inputs["L0_ffn_b2"], np.float32)

    a_ = np.arange(8)[:, None, None]
    p_ = np.arange(2)[None, :, None]
    qp = np.arange(64)[None, None, :]
    dperm = (16 * qp + 2 * a_ + p_).reshape(-1)

    def mk_wblk(w):
        out = np.zeros((DT * P, P), np.float32)
        for a in range(8):
            for pp in range(2):
                h = 2 * a + pp
                out[a * P + pp * 64:a * P + (pp + 1) * 64,
                    pp * 64:(pp + 1) * 64] = w[:, :, h]
        return out

    half = D // 2
    inv_freq = 1.0 / (10000.0 ** (np.arange(half, dtype=np.float64) / half))
    ang = np.arange(T, dtype=np.float64)[None, :] * inv_freq[:, None]
    jmap = (8 * qp[0, 0][None, :] + a_[:, 0, 0][:, None]).reshape(-1)
    cos = np.cos(ang)[jmap].astype(np.float32)
    sin = np.sin(ang)[jmap].astype(np.float32)

    tq = np.arange(P)[:, None]
    tk = np.arange(P)[None, :]
    tri = np.where(tk <= tq, 0.0, -1e9).astype(np.float32)

    in_maps = []
    for c in range(8):
        b = c // 2
        role = c % 2
        tb = role * TO
        my = np.arange(DIH) + role * DIH
        perm_u = np.concatenate([my, np.arange(DIH) + (1 - role) * DIH])

        group = [[0, 1, 2, 3], [4, 5, 6, 7]][c // 4]
        rank = group.index(c)
        pairbase = (rank // 2) * 2
        partner = rank ^ 1

        cwm = np.zeros((P, 16), np.float32)
        cwm[:, CW_SLAB_LO + pairbase] = 1.0
        cwm[:, CW_SLAB_HI + pairbase + 1] = 1.0
        cwm[:, CW_PARTNER + partner] = 1.0
        cwm[:, CW_OWN + role] = 1.0
        cwm[:, CW_MERGE_W] = 1.0 if role == 1 else 0.0
        cwm[:, CW_MERGE_B] = 0.0 if role == 1 else -1e30

        m = dict(
            x_in=np.ascontiguousarray(x[b, tb:tb + TO, :][:, dperm]),
            w_in_u=np.ascontiguousarray(W_in[dperm][:, perm_u]),
            w_in_res=np.ascontiguousarray(W_in[dperm][:, DI:][:, perm_u]),
            w_delta=np.ascontiguousarray(W_delta[perm_u][:, my]),
            b_delta_h=np.ascontiguousarray(b_delta[my].reshape(DHT, P).T),
            w_b=np.ascontiguousarray(W_B[perm_u]),
            w_c=np.ascontiguousarray(W_C[perm_u]),
            a_log_h=np.ascontiguousarray(A_log[my].reshape(DHT, P, NS).transpose(1, 0, 2).reshape(P, DHT * NS)),
            d_param=np.ascontiguousarray(D_param[perm_u].reshape(DIT, P).T),
            w_out=np.ascontiguousarray(W_out[perm_u][:, dperm]),
            ffn_w1=np.ascontiguousarray(w1[dperm]),
            ffn_b1=np.ascontiguousarray(b1.reshape(FFT, P).T),
            ffn_w2=np.ascontiguousarray(w2[:, dperm]),
            ffn_b2=np.ascontiguousarray(b2[dperm].reshape(DT, P).T),
            wqkv=np.concatenate([mk_wblk(w_q), mk_wblk(w_k), mk_wblk(w_v)], 0),
            rope_cos=cos, rope_sin=sin,
            rope_cos_own=np.ascontiguousarray(cos[:, tb:tb + TO]),
            rope_sin_own=np.ascontiguousarray(sin[:, tb:tb + TO]),
            tri_m=tri,
            id_in=np.eye(P, dtype=np.float32),
            sel16_in=np.repeat(np.eye(NS, dtype=np.float32), P, axis=1).reshape(NS, NS, P).reshape(NS, NS * P),
            cw_in=cwm,
        )
        in_maps.append(m)
    return in_maps, dperm


def kernel(**inputs):
    in_maps, dperm = _stage(inputs)
    nc = build_nc()
    res = run_bass_kernel_spmd(nc, in_maps, core_ids=list(range(8)))
    inv = np.argsort(dperm)
    out = np.empty((4, T, D), np.float32)
    for c in range(8):
        b, role = c // 2, c % 2
        o = np.asarray(res.results[c]["x_out"])
        out[b, role * TO:(role + 1) * TO, :] = o[:, inv]
    return out
